# version 23
# baseline (speedup 1.0000x reference)
import numpy as np
from contextlib import ExitStack
import concourse.bass as bass
import concourse.mybir as mybir
from concourse.bass_utils import run_bass_kernel_spmd

F32 = mybir.dt.float32
BF16 = mybir.dt.bfloat16
I32 = mybir.dt.int32
AF = mybir.ActivationFunctionType
ALU = mybir.AluOpType
AX = mybir.AxisListType

B, S, E = 4, 8192, 1024
HALF = S // 2
NCH = 64
NOWN = 32
HID = 3072
NE = 8
JH = 12
NSTR = 16
SCALE = 1.0 / 127.0
S2 = SCALE * SCALE
LN_EPS = 1e-5
SC1 = 84.0


def build_nc():
    nc = bass.Bass("TRN2", target_bir_lowering=False)

    x_own = nc.dram_tensor("x_own", [HALF, E], F32, kind="ExternalInput").ap()
    x_oth = nc.dram_tensor("x_oth", [HALF, E], F32, kind="ExternalInput").ap()
    wqkv = nc.dram_tensor("wqkv", [E, 320], F32, kind="ExternalInput").ap()
    wg_t = nc.dram_tensor("wg_t", [E, HID], I32, kind="ExternalInput").ap()
    wu_t = nc.dram_tensor("wu_t", [E, HID], I32, kind="ExternalInput").ap()
    wd_t = nc.dram_tensor("wd_t", [HID, E], I32, kind="ExternalInput").ap()
    out_d = nc.dram_tensor("out", [HALF, E], F32, kind="ExternalOutput").ap()
    xnsp = nc.dram_tensor("xnsp", [NE, 128, HALF], BF16, kind="Internal").ap()

    names = ["c_x0", "c_x1", "c_sp0", "c_sp1", "c_wq", "c_gp",
             "c_pe1", "c_a1", "c_d1", "c_e1",
             "c_nm", "c_na", "c_nd",
             "c_pa1", "c_pa2", "c_ae1", "c_ae2", "c_wc", "c_ln", "c_vh",
             "c_po", "c_do",
             "c_pg", "c_pu", "c_ge", "c_ue", "c_a3", "c_d3", "c_py", "c_ep",
             "c_wgA", "c_wuA", "c_wdA", "c_wgB", "c_wuB", "c_wdB",
             "c_xn0", "c_xn1", "c_xe0", "c_xe1", "c_out0", "c_out1",
             "c_ob", "c_oa0", "c_oa1"]
    sem = {n: nc.alloc_semaphore(n) for n in names}
    c_x = [sem["c_x0"], sem["c_x1"]]
    c_sp = [sem["c_sp0"], sem["c_sp1"]]
    c_wq, c_gp = sem["c_wq"], sem["c_gp"]
    c_pe1, c_a1, c_d1, c_e1 = (sem["c_pe1"], sem["c_a1"], sem["c_d1"],
                               sem["c_e1"])
    c_nm, c_na, c_nd = sem["c_nm"], sem["c_na"], sem["c_nd"]
    c_pa1, c_pa2 = sem["c_pa1"], sem["c_pa2"]
    c_ae1, c_ae2 = sem["c_ae1"], sem["c_ae2"]
    c_wc, c_ln = sem["c_wc"], sem["c_ln"]
    c_vh, c_po, c_do = sem["c_vh"], sem["c_po"], sem["c_do"]
    c_pg, c_pu, c_ge, c_ue = sem["c_pg"], sem["c_pu"], sem["c_ge"], sem["c_ue"]
    c_a3, c_d3, c_py, c_ep = sem["c_a3"], sem["c_d3"], sem["c_py"], sem["c_ep"]
    c_wgA, c_wuA, c_wdA = sem["c_wgA"], sem["c_wuA"], sem["c_wdA"]
    c_wgB, c_wuB, c_wdB = sem["c_wgB"], sem["c_wuB"], sem["c_wdB"]
    c_xn = [sem["c_xn0"], sem["c_xn1"]]
    c_xe = [sem["c_xe0"], sem["c_xe1"]]
    c_out = [sem["c_out0"], sem["c_out1"]]
    c_ob = sem["c_ob"]
    c_oa = [sem["c_oa0"], sem["c_oa1"]]

    def sp_par_cnt(t):
        return 16 * min(t // 2 + 1, 16)

    units = []
    pass_start = {}
    _pa = [0, 0]
    _ae = [0, 0]

    def _add_pass(pass_, p):
        dvemax = (pass_ == 1 and p % 2 == 1)
        pass_start[(pass_, p)] = len(units)
        for gg in range(16):
            kk = 2 * p + (gg % 2)
            npair = gg // 2
            rows0 = 64 * (gg % 2)
            pi = pass_ - 1
            _pa[pi] += 2
            u = dict(pass_=pass_, p=p, g=gg, kk=kk, npair=npair,
                     rows0=rows0, mm_after=_pa[pi], dvemax=dvemax,
                     exp_after=None, cons=None)
            if not dvemax:
                _ae[pi] += 1
                u["exp_after"] = _ae[pi]
                u["cons"] = ("ae1" if pass_ == 1 else "ae2", _ae[pi])
            units.append(u)

    for s in range(NSTR):
        _add_pass(1, 2 * s)
        if s >= 1:
            _add_pass(2, 2 * s - 1)
        _add_pass(1, 2 * s + 1)
        _add_pass(2, 2 * s)
    _add_pass(2, 31)

    wc_ws, wc_t1, wc_ng, wc_mx, wc_mxw = {}, {}, {}, {}, {}
    ln_cnt = {}
    _ctr = 0
    _lctr = 0
    for s in range(NSTR):
        for kk in (4 * s, 4 * s + 1):
            _ctr += 1; wc_ws[kk] = _ctr
            _lctr += 1; ln_cnt[kk] = _lctr
        for kk in (4 * s, 4 * s + 1):
            _ctr += 1; wc_t1[kk] = _ctr
            _ctr += 1; wc_ng[kk] = _ctr
        for gg in range(16):
            kk = 4 * s + 2 + (gg % 2)
            _ctr += 1; wc_mx[(kk, gg // 2)] = _ctr
        for kk in (4 * s + 2, 4 * s + 3):
            _ctr += 1; wc_mxw[kk] = _ctr
            _ctr += 1; wc_ng[kk] = _ctr

    ae1_after_chunk = {}
    ae2_after_chunk = {}
    ae2_after_pair = {}
    for u in units:
        if u["dvemax"]:
            u["cons"] = ("wc", wc_mx[(u["kk"], u["npair"])])
        elif u["pass_"] == 1:
            ae1_after_chunk[u["kk"]] = u["exp_after"]
        else:
            ae2_after_chunk[u["kk"]] = u["exp_after"]
            ae2_after_pair[u["p"]] = u["exp_after"]

    d1_red, d1_negm, d1_var, d1_mb = {}, {}, {}, {}
    _d = 0
    for it in range(NCH + 1):
        if it < NCH:
            _d += 1; d1_red[it] = _d
            _d += 1; d1_negm[it] = _d
        if it >= 1:
            _d += 1; d1_var[it - 1] = _d
            _d += 1; d1_mb[it - 1] = _d

    with ExitStack() as st0:
        ident = st0.enter_context(nc.sbuf_tensor("ident", [128, 128], BF16))
        negmt = st0.enter_context(nc.sbuf_tensor("negmt", [128, NCH], F32))
        negb = st0.enter_context(nc.sbuf_tensor("negb", [128, NCH], F32))
        gmS = st0.enter_context(nc.sbuf_tensor("gmS", [128, NCH], F32))
        sk = st0.enter_context(nc.sbuf_tensor("sk", [128, NCH], F32))
        sbias = st0.enter_context(nc.sbuf_tensor("sbias", [128, NCH], F32))
        eps86 = st0.enter_context(nc.sbuf_tensor("eps86", [128, 1], F32))
        o_bf = st0.enter_context(nc.sbuf_tensor("o_bf", [64, HALF], BF16))

        with nc.Block() as blk0:
            @blk0.gpsimd
            def _(gg):
                gg.memset(ident[:, :], 0.0).then_inc(c_gp, 1)
                gg.wait_ge(c_gp, 1)
                gg.affine_select(out=ident[:, :], in_=ident[:, :],
                                 compare_op=ALU.not_equal, fill=1.0,
                                 base=0, pattern=[[-1, 128]],
                                 channel_multiplier=1).then_inc(c_gp, 1)
                gg.memset(eps86[:, :], 5.1e-12).then_inc(c_gp, 1)
        GP0 = 3

        with ExitStack() as stA:
            q2 = stA.enter_context(nc.sbuf_tensor("q2", [128, S], BF16))
            k2 = stA.enter_context(nc.sbuf_tensor("k2", [128, S], BF16))
            v_sb = stA.enter_context(nc.sbuf_tensor("v_sb", [128, NCH * 64],
                                                    BF16))

            with ExitStack() as st1:
                wqkv_bf = st1.enter_context(
                    nc.sbuf_tensor("wqkv_bf", [128, NE * 320], BF16))
                x_sb = st1.enter_context(
                    nc.sbuf_tensor("x_sb", [128, 2 * 4096], F32))
                sq_scr = st1.enter_context(
                    nc.sbuf_tensor("sq_scr", [128, 2 * 1024], F32))
                xn_tm = st1.enter_context(
                    nc.sbuf_tensor("xn_tm", [128, 4 * 1024], BF16))
                xn_fm = st1.enter_context(
                    nc.sbuf_tensor("xn_fm", [128, 4 * 1024], BF16))
                st1t = st1.enter_context(nc.sbuf_tensor("st1t", [128, 32], F32))
                ps_tr = st1.enter_context(
                    nc.psum_tensor("ps_tr", [128, 2 * 1024], F32))
                ps_qq = st1.enter_context(nc.psum_tensor("ps_qq", [128, 128],
                                                         F32))
                ps_kk = st1.enter_context(nc.psum_tensor("ps_kk", [128, 128],
                                                         F32))
                ps_v = st1.enter_context(nc.psum_tensor("ps_v", [128, 64], F32))
                blk1 = nc.Block()
                blk1.__enter__()

                def stc(t, i):
                    r = t % 4
                    return st1t[:, r * 8 + i: r * 8 + i + 1]

                def xsl_of(t):
                    base = (t // 4 % 2) * 4096 + (t % 4) * 1024
                    return x_sb[:, base:base + 1024]

                @blk1.sync
                def _(sy):
                    for gd in range(16):
                        t0 = gd * 4
                        src = x_own if t0 < NOWN else x_oth
                        r0 = (t0 % NOWN) * 128
                        if gd >= 2:
                            sy.wait_ge(c_a1, 4 * (t0 - 5) + 4)
                            if gd <= NOWN // 4 + 1:
                                sy.wait_ge(c_xe[0], 16 * (gd - 1))
                        sy.dma_start(
                            out=x_sb[:, (gd % 2) * 4096:(gd % 2 + 1) * 4096]
                            .rearrange("p (c f) -> p c f", f=1024),
                            in_=src[r0:r0 + 512, :].rearrange(
                                "(c p) f -> p c f", p=128),
                        ).then_inc(c_x[gd % 2], 16)
                        if gd >= 1:
                            for t in range(4 * (gd - 1), 4 * (gd - 1) + 4):
                                if t < NOWN:
                                    sy.wait_ge(c_e1, 4 * t + 1)
                                    if t >= 2:
                                        sy.wait_ge(c_sp[t % 2], 16 * (t // 2))
                                    sy.dma_start(
                                        out=xnsp[:, :, t * 128:(t + 1) * 128]
                                        .rearrange("j p t -> p j t"),
                                        in_=xn_fm[:, (t % 4) * 1024:
                                                  (t % 4 + 1) * 1024]
                                        .rearrange("p (j t) -> p j t", t=128),
                                    ).then_inc(c_sp[t % 2], 16)
                        if gd < NOWN // 4:
                            sy.wait_ge(c_x[gd % 2], 16 * (gd // 2 + 1))
                            if gd >= 1:
                                sy.wait_ge(c_xe[0], 16 * gd)
                            sy.dma_start(
                                out=out_d[gd * 512:(gd + 1) * 512, :]
                                .rearrange("(c p) f -> p c f", p=128),
                                in_=x_sb[:, (gd % 2) * 4096:(gd % 2 + 1) * 4096]
                                .rearrange("p (c f) -> p c f", f=1024),
                            ).then_inc(c_xe[0], 16)

                @blk1.vector
                def _(v):
                    for it in range(NCH + 2):
                        if it < NCH:
                            t = it
                            xsl = xsl_of(t)
                            v.wait_ge(c_x[(t // 4) % 2], 16 * (t // 8 + 1))
                            if t >= 4:
                                v.wait_ge(c_a1, 4 * (t - 4) + 4)
                            v.reduce_sum(stc(t, 0), xsl, axis=AX.X,
                                         ).then_inc(c_d1, 1)
                            v.wait_ge(c_d1, d1_red[t])
                            v.tensor_scalar(stc(t, 1), stc(t, 0), -1.0 / E,
                                            None, ALU.mult).then_inc(c_d1, 1)
                        if 1 <= it <= NCH:
                            t = it - 1
                            v.wait_ge(c_a1, 4 * t + 1)
                            v.tensor_scalar(stc(t, 2), stc(t, 5), 1.0 / E,
                                            LN_EPS, ALU.mult, ALU.add,
                                            ).then_inc(c_d1, 1)
                            v.wait_ge(c_a1, 4 * t + 3)
                            v.tensor_tensor(stc(t, 4), stc(t, 1), stc(t, 3),
                                            ALU.mult).then_inc(c_d1, 1)
                        if it >= 2:
                            t = it - 2
                            v.wait_ge(c_pe1, 11 * t + 8)
                            if t >= 4:
                                v.wait_ge(c_pe1, 11 * (t - 4) + 11)
                                v.wait_ge(c_sp[t % 2], sp_par_cnt(t - 4))
                            v.tensor_copy(xn_fm[:, (t % 4) * 1024:
                                                (t % 4 + 1) * 1024],
                                          ps_tr[:, (t % 2) * 1024:
                                                (t % 2 + 1) * 1024],
                                          ).then_inc(c_e1, 1)
                            v.wait_ge(c_pe1, 11 * t + 9)
                            v.tensor_copy(q2[:, t * 128:(t + 1) * 128],
                                          ps_qq[:, :]).then_inc(c_e1, 1)
                            v.wait_ge(c_pe1, 11 * t + 10)
                            v.tensor_copy(k2[:, t * 128:(t + 1) * 128],
                                          ps_kk[:, :]).then_inc(c_e1, 1)
                            v.wait_ge(c_pe1, 11 * t + 11)
                            v.tensor_copy(v_sb[:, t * 64:(t + 1) * 64],
                                          ps_v[:, :]).then_inc(c_e1, 1)

                @blk1.scalar
                def _(s_):
                    for t in range(NCH):
                        xsl = xsl_of(t)
                        s_.wait_ge(c_x[(t // 4) % 2], 16 * (t // 8 + 1))
                        s_.wait_ge(c_d1, d1_negm[t])
                        if t >= 4:
                            s_.wait_ge(c_d1, d1_mb[t - 4])
                        if t >= 2:
                            s_.wait_ge(c_a1, 4 * (t - 2) + 4)
                        s_.activation(sq_scr[:, (t % 2) * 1024:(t % 2 + 1) * 1024],
                                      xsl, AF.Square, bias=stc(t, 1), scale=1.0,
                                      accum_out=stc(t, 5)).then_inc(c_a1, 1)
                        s_.wait_ge(c_d1, d1_var[t])
                        s_.activation(stc(t, 6), stc(t, 2), AF.Ln,
                                      ).then_inc(c_a1, 1)
                        s_.wait_ge(c_a1, 4 * t + 2)
                        s_.activation(stc(t, 3), stc(t, 6), AF.Exp, scale=-0.5,
                                      ).then_inc(c_a1, 1)
                        s_.wait_ge(c_d1, d1_mb[t])
                        s_.wait_ge(c_a1, 4 * t + 3)
                        if t >= 4:
                            s_.wait_ge(c_pe1, 11 * (t - 4) + 8)
                        s_.activation(xn_tm[:, (t % 4) * 1024:(t % 4 + 1) * 1024],
                                      xsl, AF.Identity, bias=stc(t, 4),
                                      scale=stc(t, 3)).then_inc(c_a1, 1)

                @blk1.gpsimd
                def _(gg):
                    gg.dma_start(out=wqkv_bf[:, :].rearrange(
                        "p (c n) -> p c n", n=320),
                        in_=wqkv.rearrange("(c p) n -> p c n", p=128),
                    ).then_inc(c_wq, 16)

                @blk1.tensor
                def _(t_):
                    t_.wait_ge(c_gp, 2)
                    t_.wait_ge(c_wq, 16)
                    for t in range(NCH):
                        t_.wait_ge(c_a1, 4 * t + 4)
                        if t >= 2:
                            t_.wait_ge(c_e1, 4 * (t - 2) + 1)
                        for j in range(NE):
                            t_.matmul(
                                ps_tr[:, (t % 2) * 1024 + j * 128:
                                      (t % 2) * 1024 + (j + 1) * 128],
                                xn_tm[:, (t % 4) * 1024 + j * 128:
                                      (t % 4) * 1024 + (j + 1) * 128],
                                ident[:, :], start=True, stop=True,
                            ).then_inc(c_pe1, 1)
                        t_.wait_ge(c_e1, 4 * t + 1)
                        if t >= 1:
                            t_.wait_ge(c_e1, 4 * (t - 1) + 4)
                        xfm = xn_fm[:, (t % 4) * 1024:(t % 4 + 1) * 1024]
                        for j in range(NE):
                            mm = t_.matmul(ps_qq[:, :],
                                           wqkv_bf[:, j * 320: j * 320 + 128],
                                           xfm[:, j * 128:(j + 1) * 128],
                                           start=(j == 0), stop=(j == NE - 1))
                            if j == NE - 1:
                                mm.then_inc(c_pe1, 1)
                        for j in range(NE):
                            mm = t_.matmul(ps_kk[:, :],
                                           wqkv_bf[:, j * 320 + 128:
                                                   j * 320 + 256],
                                           xfm[:, j * 128:(j + 1) * 128],
                                           start=(j == 0), stop=(j == NE - 1))
                            if j == NE - 1:
                                mm.then_inc(c_pe1, 1)
                        for j in range(NE):
                            mm = t_.matmul(ps_v[:, :],
                                           xfm[:, j * 128:(j + 1) * 128],
                                           wqkv_bf[:, j * 320 + 256:
                                                   j * 320 + 320],
                                           start=(j == 0), stop=(j == NE - 1))
                            if j == NE - 1:
                                mm.then_inc(c_pe1, 1)

                blk1.__exit__(None, None, None)

            with ExitStack() as stP:
                ident32 = stP.enter_context(
                    nc.sbuf_tensor("ident32", [128, 128], F32))
                ones_r = stP.enter_context(nc.sbuf_tensor("ones_r", [1, 128],
                                                          F32))
                nrm = stP.enter_context(nc.sbuf_tensor("nrm", [128, 3 * NCH],
                                                       F32))
                sqd = stP.enter_context(nc.sbuf_tensor("sqd", [128, 2 * 64],
                                                       F32))
                tiny = stP.enter_context(nc.sbuf_tensor("tiny", [128, 8], F32))
                trow = stP.enter_context(nc.sbuf_tensor("trow", [1, 256], F32))
                qbar = stP.enter_context(nc.sbuf_tensor("qbar", [64, 2], F32))
                qbarb = stP.enter_context(nc.sbuf_tensor("qbarb", [64, 1],
                                                         BF16))
                mc = stP.enter_context(nc.sbuf_tensor("mc", [128, NCH], F32))
                ps_tm = stP.enter_context(nc.psum_tensor("ps_tm", [128, 128],
                                                         F32))
                ps_r = stP.enter_context(nc.psum_tensor("ps_r", [1, 128], F32))
                ps_b = stP.enter_context(nc.psum_tensor("ps_b", [128, 1], F32))
                ps_mc = stP.enter_context(nc.psum_tensor("ps_mc", [128, 2],
                                                         F32))
                blk2 = nc.Block()
                blk2.__enter__()

                @blk2.gpsimd
                def _(gg):
                    gg.memset(ident32[:, :], 0.0).then_inc(c_gp, 1)
                    gg.wait_ge(c_gp, GP0 + 1)
                    gg.affine_select(out=ident32[:, :], in_=ident32[:, :],
                                     compare_op=ALU.not_equal, fill=1.0,
                                     base=0, pattern=[[-1, 128]],
                                     channel_multiplier=1).then_inc(c_gp, 1)
                    gg.memset(ones_r[0:1, :], 1.0).then_inc(c_gp, 1)

                @blk2.tensor
                def _(t_):
                    for which, src in ((0, q2), (1, k2)):
                        for kk in range(NCH):
                            u = which * NCH + kk
                            if u >= 2:
                                t_.wait_ge(c_na, u - 1)
                            t_.matmul(ps_tm[:, (u % 2) * 64:(u % 2) * 64 + 64],
                                      src[0:64, kk * 128:(kk + 1) * 128],
                                      ident[0:64, 0:64], start=True, stop=True,
                                      ).then_inc(c_nm, 1)
                    t_.wait_ge(c_nd, 1)
                    t_.wait_ge(c_gp, GP0 + 2)
                    t_.matmul(ps_r[0:1, 0:128], tiny[:, 0:1], ident32[:, :],
                              start=True, stop=True).then_inc(c_nm, 1)
                    t_.wait_ge(c_nd, 4)
                    t_.wait_ge(c_gp, GP0 + 3)
                    t_.matmul(ps_b[:, 0:1], ones_r[0:1, :], trow[0:1, 1:2],
                              start=True, stop=True).then_inc(c_nm, 1)
                    t_.wait_ge(c_nd, 7)
                    for kk in range(NCH):
                        if kk >= 2:
                            t_.wait_ge(c_nd, 9 + kk - 2 + 1)
                        t_.matmul(ps_mc[:, kk % 2: kk % 2 + 1],
                                  k2[0:64, kk * 128:(kk + 1) * 128],
                                  qbarb[0:64, 0:1], start=True, stop=True,
                                  ).then_inc(c_nm, 1)

                @blk2.scalar
                def _(s_):
                    for u in range(2 * NCH):
                        s_.wait_ge(c_nm, u + 1)
                        s_.activation(sqd[:, (u % 2) * 64:(u % 2) * 64 + 64],
                                      ps_tm[:, (u % 2) * 64:(u % 2) * 64 + 64],
                                      AF.Square,
                                      accum_out=nrm[:, u:u + 1],
                                      ).then_inc(c_na, 1)
                    s_.wait_ge(c_nd, 3)
                    s_.activation(trow[0:1, 0:1], trow[0:1, 2:3], AF.Ln,
                                  ).then_inc(c_na, 1)
                    s_.wait_ge(c_na, 2 * NCH)
                    s_.activation(nrm[:, 2 * NCH:3 * NCH], nrm[:, NCH:2 * NCH],
                                  AF.Ln).then_inc(c_na, 1)
                    s_.wait_ge(c_nd, 8)
                    s_.wait_ge(c_na, 2 * NCH + 2)
                    s_.activation(negb[:, :], nrm[:, 2 * NCH:3 * NCH], AF.Exp,
                                  scale=0.5, bias=tiny[:, 1:2],
                                  ).then_inc(c_na, 1)

                @blk2.vector
                def _(v):
                    v.wait_ge(c_na, NCH)
                    v.reduce_max(tiny[:, 0:1], nrm[:, 0:NCH], axis=AX.X,
                                 ).then_inc(c_nd, 1)
                    v.wait_ge(c_nm, 129)
                    v.tensor_copy(trow[0:1, 128:256], ps_r[0:1, 0:128],
                                  ).then_inc(c_nd, 1)
                    v.wait_ge(c_nd, 2)
                    v.reduce_max(trow[0:1, 2:3], trow[0:1, 128:256], axis=AX.X,
                                 ).then_inc(c_nd, 1)
                    v.wait_ge(c_na, 2 * NCH + 1)
                    v.tensor_scalar(trow[0:1, 1:2], trow[0:1, 0:1], 0.5, None,
                                    ALU.mult).then_inc(c_nd, 1)
                    v.reduce_sum(qbar[0:64, 0:1], q2[0:64, :], axis=AX.X,
                                 ).then_inc(c_nd, 1)
                    v.wait_ge(c_nd, 5)
                    v.tensor_scalar(qbar[0:64, 1:2], qbar[0:64, 0:1], 1.0 / S,
                                    None, ALU.mult).then_inc(c_nd, 1)
                    v.wait_ge(c_nd, 6)
                    v.tensor_copy(qbarb[0:64, 0:1], qbar[0:64, 1:2],
                                  ).then_inc(c_nd, 1)
                    v.wait_ge(c_nm, 130)
                    v.tensor_copy(tiny[:, 1:2], ps_b[:, 0:1]).then_inc(c_nd, 1)
                    for kk in range(NCH):
                        v.wait_ge(c_nm, 130 + kk + 1)
                        v.tensor_copy(mc[:, kk:kk + 1],
                                      ps_mc[:, kk % 2: kk % 2 + 1],
                                      ).then_inc(c_nd, 1)
                    v.wait_ge(c_na, 2 * NCH + 3)
                    v.wait_ge(c_nd, 72)
                    v.tensor_tensor(gmS[:, :], negb[:, :], mc[:, :],
                                    ALU.subtract).then_inc(c_nd, 1)
                    v.wait_ge(c_nd, 73)
                    v.tensor_scalar(gmS[:, :], gmS[:, :], 1e-3, None,
                                    ALU.max).then_inc(c_nd, 1)
                    v.wait_ge(c_nd, 74)
                    v.reciprocal(sk[:, :], gmS[:, :]).then_inc(c_nd, 1)
                    v.wait_ge(c_nd, 75)
                    v.tensor_scalar(sk[:, :], sk[:, :], SC1, None,
                                    ALU.mult).then_inc(c_nd, 1)
                    v.wait_ge(c_nd, 76)
                    v.tensor_scalar(gmS[:, :], gmS[:, :], 1.0 / SC1, None,
                                    ALU.mult).then_inc(c_nd, 1)
                    v.wait_ge(c_nd, 77)
                    v.tensor_tensor(sbias[:, :], negb[:, :], sk[:, :],
                                    ALU.mult).then_inc(c_nd, 1)
                    v.wait_ge(c_nd, 78)
                    v.tensor_scalar(sbias[:, :], sbias[:, :], -1.0, None,
                                    ALU.mult).then_inc(c_nd, 1)
                    v.wait_ge(c_nd, 79)
                    v.tensor_scalar(negb[:, :], negb[:, :], -1.0, None,
                                    ALU.mult).then_inc(c_nd, 1)
                    v.wait_ge(c_nd, 80)
                    v.tensor_scalar(sbias[:, :], sbias[:, :], 60.0, None,
                                    ALU.add).then_inc(c_nd, 1)
                    v.wait_ge(c_nd, 81)
                    v.scalar_tensor_tensor(negb[:, :], gmS[:, :], 60.0,
                                           negb[:, :], ALU.mult, ALU.add,
                                           ).then_inc(c_nd, 1)

                blk2.__exit__(None, None, None)

            with ExitStack() as stW:
                wg_bf = stW.enter_context(
                    nc.sbuf_tensor("wg_bf", [128, NE * JH * 128], BF16))
                wu_bf = stW.enter_context(
                    nc.sbuf_tensor("wu_bf", [128, NE * JH * 128], BF16))
                wd_bf = stW.enter_context(
                    nc.sbuf_tensor("wd_bf", [128, JH * 1024], BF16))
                xn_st = stW.enter_context(
                    nc.sbuf_tensor("xn_st", [128, 2 * 2048], BF16))
                graw = stW.enter_context(nc.sbuf_tensor("graw", [128, 3072],
                                                        F32))
                uraw = stW.enter_context(nc.sbuf_tensor("uraw", [128, 3072],
                                                        F32))
                hh = stW.enter_context(nc.sbuf_tensor("hh", [128, 3072], BF16))
                o_st = stW.enter_context(
                    nc.sbuf_tensor("o_st", [128, 2 * 2048], F32))

                def cast_w(gg, half, cwg, cwu, cwd):
                    j0 = half * JH
                    for cch in range(NE):
                        if cch >= 1:
                            gg.wait_ge(cwg, 16 * cch)
                        gg.dma_start(
                            out=wg_bf[:, cch * JH * 128:(cch + 1) * JH * 128],
                            in_=wg_t[cch * 128:(cch + 1) * 128,
                                     j0 * 128:(j0 + JH) * 128],
                        ).then_inc(cwg, 16)
                    for cch in range(NE):
                        if cch >= 1:
                            gg.wait_ge(cwu, 16 * cch)
                        gg.dma_start(
                            out=wu_bf[:, cch * JH * 128:(cch + 1) * JH * 128],
                            in_=wu_t[cch * 128:(cch + 1) * 128,
                                     j0 * 128:(j0 + JH) * 128],
                        ).then_inc(cwu, 16)
                    for gi in range(4):
                        jj = j0 + gi * 3
                        if gi >= 1:
                            gg.wait_ge(cwd, 16 * gi)
                        gg.dma_start(
                            out=wd_bf[:, gi * 3 * 1024:(gi + 1) * 3 * 1024]
                            .rearrange("p (c d) -> p c d", d=1024),
                            in_=wd_t[jj * 128:(jj + 3) * 128, :].rearrange(
                                "(c p) d -> p c d", p=128),
                        ).then_inc(cwd, 16)

                with ExitStack() as stF:
                    e_sb = stF.enter_context(
                        nc.sbuf_tensor("e_sb", [128, 2 * HALF], BF16))
                    edmp = stF.enter_context(
                        nc.sbuf_tensor("edmp", [128, 1024], BF16))
                    wp = stF.enter_context(nc.sbuf_tensor("wp", [128, 32], F32))
                    zp = stF.enter_context(nc.sbuf_tensor("zp", [128, 32], F32))
                    zz = stF.enter_context(nc.sbuf_tensor("zz", [128, 8], F32))
                    mxp = stF.enter_context(nc.sbuf_tensor("mxp", [128, 16], F32))
                    mxt = stF.enter_context(nc.sbuf_tensor("mxt", [128, 2], F32))
                    o_fm = stF.enter_context(
                        nc.sbuf_tensor("o_fm", [64, HALF], BF16))
                    vhat = stF.enter_context(
                        nc.sbuf_tensor("vhat", [128, 8 * 128], BF16))
                    ps_a = stF.enter_context(
                        nc.psum_tensor("ps_a", [128, 2048], F32))
                    ps_o = stF.enter_context(nc.psum_tensor("ps_o", [128, 512],
                                                             F32))
                    ps_gu = [stF.enter_context(
                        nc.psum_tensor("ps_gu%d" % i, [128, 512], F32))
                        for i in range(2)]
                    ps_y = stF.enter_context(nc.psum_tensor("ps_y", [128, 512],
                                                            F32))
                    blkF = nc.Block()
                    blkF.__enter__()

                    @blkF.tensor
                    def _(t_):
                        ui = [0]

                        _cons_sem = {"ae1": c_ae1, "ae2": c_ae2, "wc": c_wc}

                        def pe_unit():
                            i = ui[0]
                            u = units[i]
                            if i >= 2:
                                chn, cnt = units[i - 2]["cons"]
                                t_.wait_ge(_cons_sem[chn], cnt)
                            if u["pass_"] == 2 and u["g"] == 0:
                                t_.wait_ge(c_wc, wc_ng[2 * u["p"] + 1])
                            ch = c_pa1 if u["pass_"] == 1 else c_pa2
                            slot = i % 2
                            for hn in range(2):
                                n = u["npair"] * 2 + hn
                                t_.matmul(
                                    ps_a[:, slot * 1024 + hn * 512:
                                         slot * 1024 + (hn + 1) * 512],
                                    k2[:, u["kk"] * 128:(u["kk"] + 1) * 128],
                                    q2[:, n * 512:(n + 1) * 512],
                                    start=True, stop=True,
                                ).then_inc(ch, 1)
                            ui[0] += 1

                        def pe_gate(s, j):
                            if s == 0 and j == 0:
                                t_.wait_ge(c_wgA, 128)
                            if j == 0:
                                t_.wait_ge(c_xn[s % 2], 16 * (s // 2 + 1))
                            if 12 * s + j >= 2:
                                t_.wait_ge(c_ge, 12 * s + j - 1)
                                t_.wait_ge(c_ue, 12 * s + j - 1)
                            xs = xn_st[:, (s % 2) * 2048:(s % 2 + 1) * 2048]
                            for e in range(NE):
                                mm = t_.matmul(
                                    ps_gu[j % 2][:, 0:256],
                                    wg_bf[:, e * JH * 128 + j * 128:
                                          e * JH * 128 + (j + 1) * 128],
                                    xs[:, e * 256:(e + 1) * 256],
                                    start=(e == 0), stop=False)
                                if e == NE - 1:
                                    mm.then_inc(c_pg, 1)

                        def pe_up(s, j):
                            if s == 0 and j == 0:
                                t_.wait_ge(c_wuA, 128)
                            if 12 * s + j >= 2:
                                t_.wait_ge(c_ue, 12 * s + j - 1)
                            xs = xn_st[:, (s % 2) * 2048:(s % 2 + 1) * 2048]
                            for e in range(NE):
                                mm = t_.matmul(
                                    ps_gu[j % 2][:, 256:512],
                                    wu_bf[:, e * JH * 128 + j * 128:
                                          e * JH * 128 + (j + 1) * 128],
                                    xs[:, e * 256:(e + 1) * 256],
                                    start=False, stop=(e == NE - 1))
                                if e == NE - 1:
                                    mm.then_inc(c_pu, 1)

                        def pe_down(s, rr, j):
                            mt, dh = rr % 2, rr // 2
                            if j == 0:
                                if s == 0 and rr == 0:
                                    t_.wait_ge(c_wdA, 64)
                                t_.wait_ge(c_d3,
                                           4 * s + (1 if rr == 0 else 4))
                                if 4 * s + rr >= 1:
                                    t_.wait_ge(c_ep, 4 * s + rr)
                                if rr == 2 and s >= 1:
                                    t_.wait_ge(c_do, (2 * s - 1) * 8 + 8)
                            if rr == 0 and j in (3, 6, 9):
                                t_.wait_ge(c_d3, 4 * s + j // 3 + 1)
                            mm = t_.matmul(
                                ps_y[:, :],
                                hh[:, j * 256 + mt * 128:
                                   j * 256 + mt * 128 + 128],
                                wd_bf[:, j * 1024 + dh * 512:
                                      j * 1024 + dh * 512 + 512],
                                start=(j == 0), stop=(j == JH - 1))
                            if j == JH - 1:
                                mm.then_inc(c_py, 1)

                        def pe_oitem(p, n):
                            if n == 0:
                                t_.wait_ge(c_ae2, ae2_after_pair[p])
                                t_.wait_ge(c_vh, 3 * (2 * p + 1) + 3)
                            alt = (p % 2 == 1)
                            tgt = ps_y if (alt and n % 2 == 1) else ps_o
                            gidx = p * 8 + n
                            if alt:
                                if n == 1:
                                    t_.wait_ge(c_ep,
                                               64 if p == 31
                                               else 4 * ((p + 1) // 2) + 2)
                                if gidx >= 2:
                                    t_.wait_ge(c_do, gidx - 1)
                            else:
                                if gidx >= 1:
                                    t_.wait_ge(c_do, gidx)
                            for ci in range(2):
                                kkc = 2 * p + ci
                                mm = t_.matmul(
                                    tgt[:, :],
                                    vhat[:, (kkc % 8) * 128:(kkc % 8 + 1) * 128],
                                    e_sb[:, (kkc % 2) * HALF + n * 512:
                                         (kkc % 2) * HALF + (n + 1) * 512],
                                    start=(ci == 0), stop=(ci == 1))
                                if ci == 1:
                                    mm.then_inc(c_po, 1)

                        def weave(nunits, fillers):
                            fi = 0
                            nf = len(fillers)
                            for q in range(nunits):
                                pe_unit()
                                want = (q + 1) * nf // nunits
                                while fi < want:
                                    fillers[fi](); fi += 1
                            while fi < nf:
                                fillers[fi](); fi += 1

                        for s in range(NSTR):
                            gu1 = []
                            for j in range(6):
                                gu1.append(lambda s=s, j=j: pe_gate(s, j))
                                gu1.append(lambda s=s, j=j: pe_up(s, j))
                            gu2 = []
                            for j in range(6, JH):
                                gu2.append(lambda s=s, j=j: pe_gate(s, j))
                                gu2.append(lambda s=s, j=j: pe_up(s, j))
                            r = [[(lambda s=s, rr=rr, j=j: pe_down(s, rr, j))
                                  for j in range(JH)] for rr in range(4)]
                            if s == 0:
                                weave(16, [])
                                weave(16, [])
                                weave(16, [])
                                for f in gu1 + gu2:
                                    f()
                                for rr in range(3):
                                    for f in r[rr]:
                                        f()
                            else:
                                oprev = [(lambda p=2 * s - 2, n=n:
                                          pe_oitem(p, n)) for n in range(8)]
                                ocur = [(lambda p=2 * s - 1, n=n:
                                         pe_oitem(p, n)) for n in range(8)]
                                r3prev = [(lambda sp=s - 1, j=j:
                                           pe_down(sp, 3, j))
                                          for j in range(JH)]
                                seg1f = (gu1[0:2] + oprev[0:4] + gu1[2:4]
                                         + oprev[4:8] + gu1[4:] + r3prev)
                                weave(16, seg1f)
                                weave(16, gu2)
                                for f in r[0]:
                                    f()
                                weave(16, r[1])
                                for f in ocur:
                                    f()
                                weave(16, r[2])
                        for n in range(8):
                            pe_oitem(30, n)
                        for j in range(JH):
                            pe_down(NSTR - 1, 3, j)
                        for q in range(16):
                            pe_unit()
                        for n in range(8):
                            pe_oitem(31, n)

                    @blkF.scalar
                    def _(s_):
                        prev_exp = [None]

                        def act_exp(i):
                            u = units[i]
                            slot = i % 2
                            kk, npair = u["kk"], u["npair"]
                            if prev_exp[0] is not None:
                                s_.wait_ge(*prev_exp[0])
                            prev_exp[0] = (c_ae1 if u["pass_"] == 1 else c_ae2,
                                           u["exp_after"])
                            if u["pass_"] == 1:
                                s_.wait_ge(c_pa1, u["mm_after"])
                                if kk >= 4 and npair == 0:
                                    s_.wait_ge(c_wc, wc_ws[kk - 4])
                                s_.activation(
                                    edmp[:, :],
                                    ps_a[:, slot * 1024:(slot + 1) * 1024],
                                    AF.Exp, scale=sk[:, kk:kk + 1],
                                    bias=sbias[:, kk:kk + 1],
                                    accum_out=wp[:, (kk % 4) * 8 + npair:
                                                 (kk % 4) * 8 + npair + 1],
                                ).then_inc(c_ae1, 1)
                            else:
                                s_.wait_ge(c_pa2, u["mm_after"])
                                if npair == 0:
                                    if kk >= 2:
                                        s_.wait_ge(c_po, 8 * (kk // 2))
                                    if kk >= 4:
                                        s_.wait_ge(c_vh, 3 * (kk - 4) + 1)
                                if npair < 4:
                                    out = e_sb[:, (kk % 2) * HALF + npair * 1024:
                                               (kk % 2) * HALF
                                               + (npair + 1) * 1024]
                                else:
                                    out = edmp[:, :]
                                s_.activation(
                                    out, ps_a[:, slot * 1024:(slot + 1) * 1024],
                                    AF.Exp, scale=1.0, bias=negmt[:, kk:kk + 1],
                                    accum_out=zp[:, (kk % 4) * 8 + npair:
                                                 (kk % 4) * 8 + npair + 1],
                                ).then_inc(c_ae2, 1)

                        def act_ln(kk):
                            s_.wait_ge(c_wc, wc_ws[kk])
                            s_.activation(zz[:, 4 + kk % 2: 5 + kk % 2],
                                          zz[:, 4 + kk % 2: 5 + kk % 2],
                                          AF.Ln, bias=eps86[:, 0:1],
                                          ).then_inc(c_ln, 1)

                        def act_silu(s, h):
                            s_.wait_ge(c_ge, 12 * s + 3 * (h + 1))
                            s_.activation(graw[:, h * 768:(h + 1) * 768],
                                          graw[:, h * 768:(h + 1) * 768],
                                          AF.Silu, scale=SCALE,
                                          ).then_inc(c_a3, 1)

                        def exps_of(pass_, p):
                            i0 = pass_start[(pass_, p)]
                            for q in range(16):
                                act_exp(i0 + q)

                        for s in range(NSTR):
                            exps_of(1, 2 * s)
                            act_ln(4 * s); act_ln(4 * s + 1)
                            if s >= 1:
                                exps_of(2, 2 * s - 1)
                                for hq in range(4):
                                    act_silu(s, hq)
                                exps_of(2, 2 * s)
                            else:
                                exps_of(2, 0)
                                for hq in range(4):
                                    act_silu(0, hq)
                        exps_of(2, 31)

                    @blkF.vector
                    def _(v):
                        def evg(s, j):
                            v.wait_ge(c_pg, 12 * s + j + 1)
                            v.wait_ge(c_pu, 12 * s + j + 1)
                            if j == 0 and s >= 1:
                                v.wait_ge(c_a3, 4 * s)
                                v.wait_ge(c_d3, 4 * s)
                            v.tensor_copy(graw[:, j * 256:(j + 1) * 256],
                                          ps_gu[j % 2][:, 0:256],
                                          ).then_inc(c_ge, 1)

                        def evu(s, j):
                            v.wait_ge(c_pu, 12 * s + j + 1)
                            if j == 0 and s >= 1:
                                v.wait_ge(c_d3, 4 * s)
                            v.tensor_copy(uraw[:, j * 256:(j + 1) * 256],
                                          ps_gu[j % 2][:, 256:512],
                                          ).then_inc(c_ue, 1)

                        def wsum(kk):
                            v.wait_ge(c_ae1, ae1_after_chunk[kk])
                            if kk >= 4:
                                v.wait_ge(c_wc, wc_t1[kk - 4])
                                v.wait_ge(c_ln, ln_cnt[kk - 4])
                            v.reduce_sum(zz[:, 4 + kk % 2: 5 + kk % 2],
                                         wp[:, (kk % 4) * 8:(kk % 4) * 8 + 8],
                                         axis=AX.X).then_inc(c_wc, 1)

                        def negm(kk):
                            v.wait_ge(c_ln, ln_cnt[kk])
                            if kk >= 4:
                                v.wait_ge(c_wc, wc_ng[kk - 4])
                            v.tensor_tensor(zz[:, 6 + kk % 2: 7 + kk % 2],
                                            zz[:, 4 + kk % 2: 5 + kk % 2],
                                            gmS[:, kk:kk + 1], ALU.mult,
                                            ).then_inc(c_wc, 1)
                            v.wait_ge(c_wc, wc_t1[kk])
                            v.tensor_tensor(negmt[:, kk:kk + 1],
                                            negb[:, kk:kk + 1],
                                            zz[:, 6 + kk % 2: 7 + kk % 2],
                                            ALU.subtract).then_inc(c_wc, 1)

                        def dmax(kk, npair):
                            i = pass_start[(1, kk // 2)] + npair * 2 + (kk % 2)
                            v.wait_ge(c_pa1, units[i]["mm_after"])
                            if kk >= 6:
                                v.wait_ge(c_wc, wc_mxw[kk - 4])
                            v.reduce_max(mxp[:, (kk % 2) * 8 + npair:
                                              (kk % 2) * 8 + npair + 1],
                                         ps_a[:, (i % 2) * 1024:
                                              (i % 2 + 1) * 1024],
                                         axis=AX.X).then_inc(c_wc, 1)

                        def mxw_op(kk):
                            v.wait_ge(c_wc, wc_mx[(kk, 7)])
                            v.reduce_max(mxt[:, kk % 2: kk % 2 + 1],
                                         mxp[:, (kk % 2) * 8:(kk % 2) * 8 + 8],
                                         axis=AX.X).then_inc(c_wc, 1)

                        def ngD(kk):
                            v.wait_ge(c_wc, wc_mxw[kk])
                            v.tensor_scalar(negmt[:, kk:kk + 1],
                                            mxt[:, kk % 2: kk % 2 + 1], -1.0,
                                            None, ALU.mult).then_inc(c_wc, 1)

                        def dmax_pass(p):
                            for gg in range(16):
                                dmax(2 * p + (gg % 2), gg // 2)
                            mxw_op(2 * p); ngD(2 * p)
                            mxw_op(2 * p + 1); ngD(2 * p + 1)

                        def zchain(kk):
                            v.wait_ge(c_ae2, ae2_after_chunk[kk])
                            if kk >= 2:
                                v.wait_ge(c_vh, 3 * (kk - 2) + 2)
                            v.reduce_sum(zz[:, kk % 2: kk % 2 + 1],
                                         zp[:, (kk % 4) * 8:(kk % 4) * 8 + 8],
                                         axis=AX.X).then_inc(c_vh, 1)
                            v.wait_ge(c_vh, 3 * kk + 1)
                            v.reciprocal(zz[:, 2 + kk % 2: 3 + kk % 2],
                                         zz[:, kk % 2: kk % 2 + 1],
                                         ).then_inc(c_vh, 1)
                            v.wait_ge(c_vh, 3 * kk + 2)
                            if kk >= 8:
                                v.wait_ge(c_po, 8 * ((kk - 8) // 2 + 1))
                            else:
                                v.wait_ge(c_gp, GP0 + 4)
                            v.tensor_scalar(vhat[:, (kk % 8) * 128:
                                                 (kk % 8) * 128 + 64],
                                            v_sb[:, kk * 64:(kk + 1) * 64],
                                            zz[:, 2 + kk % 2: 3 + kk % 2], None,
                                            ALU.mult).then_inc(c_vh, 1)

                        def drain1(p, n):
                            v.wait_ge(c_po, p * 8 + n + 1)
                            srcb = (ps_y if (p % 2 == 1 and n % 2 == 1)
                                    else ps_o)
                            if p == 0:
                                v.tensor_copy(
                                    o_fm[0:64, n * 512:(n + 1) * 512],
                                    srcb[0:64, :]).then_inc(c_do, 1)
                            else:
                                v.tensor_tensor(
                                    o_fm[0:64, n * 512:(n + 1) * 512],
                                    o_fm[0:64, n * 512:(n + 1) * 512],
                                    srcb[0:64, :], ALU.add,
                                    ).then_inc(c_do, 1)

                        def drains(p):
                            for n in range(8):
                                drain1(p, n)

                        def hhop(s, h):
                            v.wait_ge(c_a3, 4 * s + h + 1)
                            v.wait_ge(c_ue, 12 * s + 3 * (h + 1))
                            if s >= 1 and h == 0:
                                v.wait_ge(c_py, 4 * s)
                            v.tensor_tensor(hh[:, h * 768:(h + 1) * 768],
                                            graw[:, h * 768:(h + 1) * 768],
                                            uraw[:, h * 768:(h + 1) * 768],
                                            ALU.mult).then_inc(c_d3, 1)

                        def stt_A(s, rr):
                            mt, dh = rr % 2, rr // 2
                            v.wait_ge(c_py, 4 * s + rr + 1)
                            if s >= 2:
                                v.wait_ge(c_out[s % 2], 16 * (s // 2))
                            off = (s % 2) * 2048 + mt * 1024 + dh * 512
                            v.tensor_scalar(
                                o_st[:, off:off + 512], ps_y[:, :], S2,
                                None, ALU.mult).then_inc(c_ep, 1)

                        def drains2(p, half):
                            for n in (range(4) if half == 0 else range(4, 8)):
                                drain1(p, n)

                        for s in range(NSTR):
                            if s == 0:
                                wsum(0); wsum(1)
                                negm(0); negm(1)
                                dmax_pass(1)
                                zchain(0); zchain(1)
                                for j in range(JH):
                                    evg(0, j); evu(0, j)
                                for hq in range(4):
                                    hhop(0, hq)
                                stt_A(0, 0); stt_A(0, 1)
                                stt_A(0, 2)
                            else:
                                evg(s, 0); evu(s, 0)
                                drains2(2 * s - 2, 0)
                                evg(s, 1); evu(s, 1)
                                drains2(2 * s - 2, 1)
                                for j in range(2, 6):
                                    evg(s, j); evu(s, j)
                                wsum(4 * s); wsum(4 * s + 1)
                                stt_A(s - 1, 3)
                                for j in range(6, JH):
                                    evg(s, j); evu(s, j)
                                zchain(4 * s - 2); zchain(4 * s - 1)
                                hhop(s, 0); hhop(s, 1)
                                negm(4 * s); negm(4 * s + 1)
                                hhop(s, 2); hhop(s, 3)
                                stt_A(s, 0)
                                dmax_pass(2 * s + 1)
                                stt_A(s, 1)
                                drains(2 * s - 1)
                                zchain(4 * s); zchain(4 * s + 1)
                                stt_A(s, 2)
                        drains(30)
                        stt_A(NSTR - 1, 3)
                        zchain(62); zchain(63)
                        drains(31)
                        v.wait_ge(c_do, 256)
                        v.tensor_copy(o_bf[0:64, :], o_fm[0:64, :],
                                      ).then_inc(c_do, 1)

                    @blkF.gpsimd
                    def _(gg):
                        gg.memset(vhat[:, :], 0.0).then_inc(c_gp, 1)
                        cast_w(gg, 0, c_wgA, c_wuA, c_wdA)
                        gg.wait_ge(c_xe[0], 64)
                        for s in range(NSTR):
                            gg.wait_ge(c_ep, 4 * s + 4)
                            gg.dma_start(
                                out=out_d[s * 256:(s + 1) * 256, :].rearrange(
                                    "(c p) f -> p c f", p=128),
                                in_=o_st[:, (s % 2) * 2048:(s % 2 + 1) * 2048]
                                .rearrange("p (c f) -> p c f", f=1024),
                                accum_op=ALU.add,
                            ).then_inc(c_out[s % 2], 16)

                    @blkF.sync
                    def _(sy):
                        sy.wait_ge(c_sp[0], 256)
                        sy.wait_ge(c_sp[1], 256)
                        for s in range(NSTR):
                            if s >= 2:
                                sy.wait_ge(c_pu, 12 * (s - 2) + 12)
                                sy.wait_ge(c_xn[s % 2], 16 * (s // 2))
                            sy.dma_start(
                                out=xn_st[:, (s % 2) * 2048:(s % 2 + 1) * 2048]
                                .rearrange("p (j t) -> p j t", t=256),
                                in_=xnsp[:, :, s * 256:(s + 1) * 256].rearrange(
                                    "j p t -> p j t"),
                            ).then_inc(c_xn[s % 2], 16)

                    blkF.__exit__(None, None, None)

                with ExitStack() as stB:
                    o_tm = stB.enter_context(
                        nc.sbuf_tensor("o_tm", [128, 2048], F32))
                    o_stB = stB.enter_context(
                        nc.sbuf_tensor("o_stB", [128, 2 * 2048], F32))
                    ps_gB = [stB.enter_context(
                        nc.psum_tensor("ps_gB%d" % i, [128, 256], F32))
                        for i in range(2)]
                    ps_uB = [stB.enter_context(
                        nc.psum_tensor("ps_uB%d" % i, [128, 256], F32))
                        for i in range(2)]
                    ps_yB = [stB.enter_context(
                        nc.psum_tensor("ps_yB%d" % i, [128, 512], F32))
                        for i in range(2)]
                    ps_ot = [stB.enter_context(
                        nc.psum_tensor("ps_ot%d" % i, [128, 64], F32))
                        for i in range(2)]
                    blkB = nc.Block()
                    blkB.__enter__()

                    @blkB.gpsimd
                    def _(gg):
                        cast_w(gg, 1, c_wgB, c_wuB, c_wdB)
                        for s in range(NSTR):
                            gg.wait_ge(c_ep, 64 + 4 * s + 4)
                            gg.wait_ge(c_out[s % 2], 16 * (s // 2 + 1))
                            gg.dma_start(
                                out=out_d[s * 256:(s + 1) * 256, :].rearrange(
                                    "(c p) f -> p c f", p=128),
                                in_=o_stB[:, (s % 2) * 2048:(s % 2 + 1) * 2048]
                                .rearrange("p (c f) -> p c f", f=1024),
                                accum_op=ALU.add,
                            ).then_inc(c_oa[s % 2], 16)
                        gg.wait_ge(c_oa[0], 128)
                        gg.wait_ge(c_oa[1], 128)

                    @blkB.tensor
                    def _(t_):
                        for c in range(32):
                            if c >= 2:
                                t_.wait_ge(c_ob, c - 1)
                            t_.matmul(ps_ot[c % 2][:, :],
                                      o_bf[0:64, c * 128:(c + 1) * 128],
                                      ident[0:64, 0:64], start=True, stop=True,
                                      ).then_inc(c_po, 1)
                        for s in range(NSTR):
                            t_.wait_ge(c_xn[s % 2], 16 * (8 + s // 2 + 1))
                            xs = xn_st[:, (s % 2) * 2048:(s % 2 + 1) * 2048]
                            for j in range(JH):
                                if 12 * s + j >= 2:
                                    t_.wait_ge(c_ge, 192 + 12 * s + j - 1)
                                for e in range(NE):
                                    if s == 0 and j == 0:
                                        t_.wait_ge(c_wgB, 16 * (e + 1))
                                    mm = t_.matmul(
                                        ps_gB[j % 2][:, :],
                                        wg_bf[:, e * JH * 128 + j * 128:
                                              e * JH * 128 + (j + 1) * 128],
                                        xs[:, e * 256:(e + 1) * 256],
                                        start=(e == 0), stop=(e == NE - 1))
                                    if e == NE - 1:
                                        mm.then_inc(c_pg, 1)
                                if 12 * s + j >= 2:
                                    t_.wait_ge(c_ue, 192 + 12 * s + j - 1)
                                for e in range(NE):
                                    if s == 0 and j == 0:
                                        t_.wait_ge(c_wuB, 16 * (e + 1))
                                    mm = t_.matmul(
                                        ps_uB[j % 2][:, :],
                                        wu_bf[:, e * JH * 128 + j * 128:
                                              e * JH * 128 + (j + 1) * 128],
                                        xs[:, e * 256:(e + 1) * 256],
                                        start=(e == 0), stop=(e == NE - 1))
                                    if e == NE - 1:
                                        mm.then_inc(c_pu, 1)
                            for rr in range(4):
                                mt, dh = rr % 2, rr // 2
                                for j in range(JH):
                                    if s == 0 and rr == 0:
                                        t_.wait_ge(c_wdB, 16 * (j // 3 + 1))
                                    if j == 0:
                                        t_.wait_ge(c_d3, 64 + 4 * s
                                                   + (1 if rr == 0 else 4))
                                        if 4 * s + rr >= 1:
                                            t_.wait_ge(c_ep,
                                                       64 + 4 * s + rr - 1)
                                    if rr == 0 and j in (3, 6, 9):
                                        t_.wait_ge(c_d3,
                                                   64 + 4 * s + j // 3 + 1)
                                    mm = t_.matmul(
                                        ps_yB[rr % 2][:, :],
                                        hh[:, j * 256 + mt * 128:
                                           j * 256 + mt * 128 + 128],
                                        wd_bf[:, j * 1024 + dh * 512:
                                              j * 1024 + dh * 512 + 512],
                                        start=(j == 0), stop=(j == JH - 1))
                                    if j == JH - 1:
                                        mm.then_inc(c_py, 1)

                    @blkB.scalar
                    def _(s_):
                        for s in range(NSTR):
                            for h in range(4):
                                s_.wait_ge(c_ge, 192 + 12 * s + 3 * (h + 1))
                                s_.activation(
                                    graw[:, h * 768:(h + 1) * 768],
                                    graw[:, h * 768:(h + 1) * 768],
                                    AF.Silu, scale=SCALE).then_inc(c_a3, 1)

                    @blkB.vector
                    def _(v):
                        for c in range(32):
                            v.wait_ge(c_po, 256 + c + 1)
                            v.tensor_copy(o_tm[:, c * 64:(c + 1) * 64],
                                          ps_ot[c % 2][:, :]).then_inc(c_ob, 1)
                        for s in range(NSTR):
                            for j in range(JH):
                                v.wait_ge(c_pg, 192 + 12 * s + j + 1)
                                if j == 0 and s >= 1:
                                    v.wait_ge(c_a3, 64 + 4 * s)
                                    v.wait_ge(c_d3, 64 + 4 * s)
                                v.tensor_copy(graw[:, j * 256:(j + 1) * 256],
                                              ps_gB[j % 2][:, :],
                                              ).then_inc(c_ge, 1)
                                v.wait_ge(c_pu, 192 + 12 * s + j + 1)
                                if j == 0 and s >= 1:
                                    v.wait_ge(c_d3, 64 + 4 * s)
                                v.tensor_copy(uraw[:, j * 256:(j + 1) * 256],
                                              ps_uB[j % 2][:, :],
                                              ).then_inc(c_ue, 1)
                                if j in (2, 5, 8, 11):
                                    h = j // 3
                                    v.wait_ge(c_a3, 64 + 4 * s + h + 1)
                                    v.wait_ge(c_ue, 192 + 12 * s + 3 * (h + 1))
                                    if s >= 1 and h == 0:
                                        v.wait_ge(c_py, 64 + 4 * s)
                                    v.tensor_tensor(
                                        hh[:, h * 768:(h + 1) * 768],
                                        graw[:, h * 768:(h + 1) * 768],
                                        uraw[:, h * 768:(h + 1) * 768],
                                        ALU.mult).then_inc(c_d3, 1)
                            for rr in range(4):
                                mt, dh = rr % 2, rr // 2
                                v.wait_ge(c_py, 64 + 4 * s + rr + 1)
                                if s == 0 and rr == 0:
                                    v.wait_ge(c_ob, 32)
                                if s >= 2:
                                    v.wait_ge(c_oa[s % 2], 16 * (s // 2))
                                ch = 2 * s + mt
                                off = (s % 2) * 2048 + mt * 1024 + dh * 512
                                v.scalar_tensor_tensor(
                                    o_stB[:, off:off + 512].rearrange(
                                        "p (r d) -> p r d", d=64),
                                    ps_yB[rr % 2][:, :].rearrange(
                                        "p (r d) -> p r d", d=64),
                                    S2,
                                    o_tm[:, ch * 64:(ch + 1) * 64].rearrange(
                                        "p (r d) -> p r d", r=1)
                                    .broadcast_to([128, 8, 64]),
                                    ALU.mult, ALU.add).then_inc(c_ep, 1)

                    @blkB.sync
                    def _(sy):
                        for s in range(NSTR):
                            if s >= 2:
                                sy.wait_ge(c_pu, 192 + 12 * (s - 2) + 12)
                            sy.wait_ge(c_xn[s % 2], 16 * (8 + s // 2))
                            sy.dma_start(
                                out=xn_st[:, (s % 2) * 2048:(s % 2 + 1) * 2048]
                                .rearrange("p (j t) -> p j t", t=256),
                                in_=xnsp[:, :, s * 256:(s + 1) * 256].rearrange(
                                    "j p t -> p j t"),
                            ).then_inc(c_xn[s % 2], 16)

                    blkB.__exit__(None, None, None)

    return nc


_NC_CACHE = {}


def kernel(L0_attn_x, L0_attn_Wq, L0_attn_Wk, L0_attn_Wv,
           L0_ffn_w_gate, L0_ffn_w_up, L0_ffn_w_down):
    x = np.asarray(L0_attn_x, dtype=np.float32)
    wq = np.asarray(L0_attn_Wq, dtype=np.float32)
    wk = np.asarray(L0_attn_Wk, dtype=np.float32)
    wv = np.asarray(L0_attn_Wv, dtype=np.float32)
    wg = np.asarray(L0_ffn_w_gate, dtype=np.int32)
    wu = np.asarray(L0_ffn_w_up, dtype=np.int32)
    wd = np.asarray(L0_ffn_w_down, dtype=np.int32)

    wqkv = np.ascontiguousarray(
        np.concatenate([wq.T, wq.T, wk.T, np.zeros_like(wk.T), wv.T], axis=1))
    wg_t = np.ascontiguousarray(wg.T)
    wu_t = np.ascontiguousarray(wu.T)
    wd_t = np.ascontiguousarray(wd.T)

    if "nc" not in _NC_CACHE:
        _NC_CACHE["nc"] = build_nc()
    nc = _NC_CACHE["nc"]

    in_maps = []
    for c in range(8):
        b, h = c // 2, c % 2
        xo = np.ascontiguousarray(x[b, h * HALF:(h + 1) * HALF, :])
        xt = np.ascontiguousarray(x[b, (1 - h) * HALF:(2 - h) * HALF, :])
        in_maps.append({
            "x_own": xo, "x_oth": xt, "wqkv": wqkv,
            "wg_t": wg_t, "wu_t": wu_t, "wd_t": wd_t,
        })
    res = run_bass_kernel_spmd(nc, in_maps, core_ids=list(range(8)))
    out = np.empty((B, S, E), dtype=np.float32)
    for c in range(8):
        b, h = c // 2, c % 2
        out[b, h * HALF:(h + 1) * HALF, :] = res.results[c]["out"]
    return out


# version 24
# speedup vs baseline: 1.0232x; 1.0232x over previous
import numpy as np
from contextlib import ExitStack
import concourse.bass as bass
import concourse.mybir as mybir
from concourse.bass_utils import run_bass_kernel_spmd

F32 = mybir.dt.float32
BF16 = mybir.dt.bfloat16
I32 = mybir.dt.int32
AF = mybir.ActivationFunctionType
ALU = mybir.AluOpType
AX = mybir.AxisListType

B, S, E = 4, 8192, 1024
HALF = S // 2
NCH = 64
NOWN = 32
HID = 3072
NE = 8
JH = 12
NSTR = 16
SCALE = 1.0 / 127.0
S2 = SCALE * SCALE
LN_EPS = 1e-5
SC1 = 84.0


def build_nc():
    nc = bass.Bass("TRN2", target_bir_lowering=False)

    x_own = nc.dram_tensor("x_own", [HALF, E], F32, kind="ExternalInput").ap()
    x_oth = nc.dram_tensor("x_oth", [HALF, E], F32, kind="ExternalInput").ap()
    wqkv = nc.dram_tensor("wqkv", [E, 320], F32, kind="ExternalInput").ap()
    wg_t = nc.dram_tensor("wg_t", [E, HID], I32, kind="ExternalInput").ap()
    wu_t = nc.dram_tensor("wu_t", [E, HID], I32, kind="ExternalInput").ap()
    wd_t = nc.dram_tensor("wd_t", [HID, E], I32, kind="ExternalInput").ap()
    out_d = nc.dram_tensor("out", [HALF, E], F32, kind="ExternalOutput").ap()
    xnsp = nc.dram_tensor("xnsp", [NE, 128, HALF], BF16, kind="Internal").ap()

    names = ["c_x0", "c_x1", "c_sp0", "c_sp1", "c_wq", "c_gp",
             "c_pe1", "c_a1", "c_d1", "c_e1",
             "c_nm", "c_na", "c_nd",
             "c_pa1", "c_pa2", "c_ae1", "c_ae2", "c_wc", "c_ln", "c_vh",
             "c_po", "c_do",
             "c_pg", "c_pu", "c_ge", "c_ue", "c_a3", "c_d3", "c_py", "c_ep",
             "c_wgA", "c_wuA", "c_wdA", "c_wgB", "c_wuB", "c_wdB",
             "c_xn0", "c_xn1", "c_xe0", "c_xe1", "c_out0", "c_out1",
             "c_ob", "c_oa0", "c_oa1"]
    sem = {n: nc.alloc_semaphore(n) for n in names}
    c_x = [sem["c_x0"], sem["c_x1"]]
    c_sp = [sem["c_sp0"], sem["c_sp1"]]
    c_wq, c_gp = sem["c_wq"], sem["c_gp"]
    c_pe1, c_a1, c_d1, c_e1 = (sem["c_pe1"], sem["c_a1"], sem["c_d1"],
                               sem["c_e1"])
    c_nm, c_na, c_nd = sem["c_nm"], sem["c_na"], sem["c_nd"]
    c_pa1, c_pa2 = sem["c_pa1"], sem["c_pa2"]
    c_ae1, c_ae2 = sem["c_ae1"], sem["c_ae2"]
    c_wc, c_ln = sem["c_wc"], sem["c_ln"]
    c_vh, c_po, c_do = sem["c_vh"], sem["c_po"], sem["c_do"]
    c_pg, c_pu, c_ge, c_ue = sem["c_pg"], sem["c_pu"], sem["c_ge"], sem["c_ue"]
    c_a3, c_d3, c_py, c_ep = sem["c_a3"], sem["c_d3"], sem["c_py"], sem["c_ep"]
    c_wgA, c_wuA, c_wdA = sem["c_wgA"], sem["c_wuA"], sem["c_wdA"]
    c_wgB, c_wuB, c_wdB = sem["c_wgB"], sem["c_wuB"], sem["c_wdB"]
    c_xn = [sem["c_xn0"], sem["c_xn1"]]
    c_xe = [sem["c_xe0"], sem["c_xe1"]]
    c_out = [sem["c_out0"], sem["c_out1"]]
    c_ob = sem["c_ob"]
    c_oa = [sem["c_oa0"], sem["c_oa1"]]

    def sp_par_cnt(t):
        return 16 * min(t // 2 + 1, 16)

    units = []
    pass_start = {}
    _pa = [0, 0]
    _ae = [0, 0]

    def _add_pass(pass_, p):
        dvemax = (pass_ == 1 and p % 2 == 1)
        pass_start[(pass_, p)] = len(units)
        for gg in range(16):
            kk = 2 * p + (gg % 2)
            npair = gg // 2
            rows0 = 64 * (gg % 2)
            pi = pass_ - 1
            _pa[pi] += 2
            u = dict(pass_=pass_, p=p, g=gg, kk=kk, npair=npair,
                     rows0=rows0, mm_after=_pa[pi], dvemax=dvemax,
                     exp_after=None, cons=None)
            if not dvemax:
                _ae[pi] += 1
                u["exp_after"] = _ae[pi]
                u["cons"] = ("ae1" if pass_ == 1 else "ae2", _ae[pi])
            units.append(u)

    for s in range(NSTR):
        _add_pass(1, 2 * s)
        if s >= 1:
            _add_pass(2, 2 * s - 1)
        _add_pass(1, 2 * s + 1)
        _add_pass(2, 2 * s)
    _add_pass(2, 31)

    wc_ws, wc_t1, wc_ng, wc_mx, wc_mxw = {}, {}, {}, {}, {}
    ln_cnt = {}
    _ctr = 0
    _lctr = 0
    for s in range(NSTR):
        for kk in (4 * s, 4 * s + 1):
            _ctr += 1; wc_ws[kk] = _ctr
            _lctr += 1; ln_cnt[kk] = _lctr
        for kk in (4 * s, 4 * s + 1):
            _ctr += 1; wc_t1[kk] = _ctr
            _ctr += 1; wc_ng[kk] = _ctr
        for gg in range(16):
            kk = 4 * s + 2 + (gg % 2)
            _ctr += 1; wc_mx[(kk, gg // 2)] = _ctr
        for kk in (4 * s + 2, 4 * s + 3):
            _ctr += 1; wc_mxw[kk] = _ctr
            _ctr += 1; wc_ng[kk] = _ctr

    ae1_after_chunk = {}
    ae2_after_chunk = {}
    ae2_after_pair = {}
    for u in units:
        if u["dvemax"]:
            u["cons"] = ("wc", wc_mx[(u["kk"], u["npair"])])
        elif u["pass_"] == 1:
            ae1_after_chunk[u["kk"]] = u["exp_after"]
        else:
            ae2_after_chunk[u["kk"]] = u["exp_after"]
            ae2_after_pair[u["p"]] = u["exp_after"]

    d1_red, d1_negm, d1_var, d1_mb = {}, {}, {}, {}
    _d = 0
    for it in range(NCH + 1):
        if it < NCH:
            _d += 1; d1_red[it] = _d
            _d += 1; d1_negm[it] = _d
        if it >= 1:
            _d += 1; d1_var[it - 1] = _d
            _d += 1; d1_mb[it - 1] = _d

    with ExitStack() as st0:
        ident = st0.enter_context(nc.sbuf_tensor("ident", [128, 128], BF16))
        negmt = st0.enter_context(nc.sbuf_tensor("negmt", [128, NCH], F32))
        negb = st0.enter_context(nc.sbuf_tensor("negb", [128, NCH], F32))
        gmS = st0.enter_context(nc.sbuf_tensor("gmS", [128, NCH], F32))
        sk = st0.enter_context(nc.sbuf_tensor("sk", [128, NCH], F32))
        sbias = st0.enter_context(nc.sbuf_tensor("sbias", [128, NCH], F32))
        eps86 = st0.enter_context(nc.sbuf_tensor("eps86", [128, 1], F32))
        o_bf = st0.enter_context(nc.sbuf_tensor("o_bf", [64, HALF], BF16))

        with nc.Block() as blk0:
            @blk0.gpsimd
            def _(gg):
                gg.memset(ident[:, :], 0.0).then_inc(c_gp, 1)
                gg.wait_ge(c_gp, 1)
                gg.affine_select(out=ident[:, :], in_=ident[:, :],
                                 compare_op=ALU.not_equal, fill=1.0,
                                 base=0, pattern=[[-1, 128]],
                                 channel_multiplier=1).then_inc(c_gp, 1)
                gg.memset(eps86[:, :], 5.1e-12).then_inc(c_gp, 1)
        GP0 = 3

        with ExitStack() as stA:
            q2 = stA.enter_context(nc.sbuf_tensor("q2", [128, S], BF16))
            k2 = stA.enter_context(nc.sbuf_tensor("k2", [128, S], BF16))
            v_sb = stA.enter_context(nc.sbuf_tensor("v_sb", [128, NCH * 64],
                                                    BF16))

            with ExitStack() as st1:
                wqkv_bf = st1.enter_context(
                    nc.sbuf_tensor("wqkv_bf", [128, NE * 320], BF16))
                x_sb = st1.enter_context(
                    nc.sbuf_tensor("x_sb", [128, 2 * 4096], F32))
                sq_scr = st1.enter_context(
                    nc.sbuf_tensor("sq_scr", [128, 2 * 1024], F32))
                xn_tm = st1.enter_context(
                    nc.sbuf_tensor("xn_tm", [128, 4 * 1024], BF16))
                xn_fm = st1.enter_context(
                    nc.sbuf_tensor("xn_fm", [128, 4 * 1024], BF16))
                st1t = st1.enter_context(nc.sbuf_tensor("st1t", [128, 32], F32))
                ps_tr = st1.enter_context(
                    nc.psum_tensor("ps_tr", [128, 2 * 1024], F32))
                ps_qq = st1.enter_context(nc.psum_tensor("ps_qq", [128, 128],
                                                         F32))
                ps_kk = st1.enter_context(nc.psum_tensor("ps_kk", [128, 128],
                                                         F32))
                ps_v = st1.enter_context(nc.psum_tensor("ps_v", [128, 64], F32))
                blk1 = nc.Block()
                blk1.__enter__()

                def stc(t, i):
                    r = t % 4
                    return st1t[:, r * 8 + i: r * 8 + i + 1]

                def xsl_of(t):
                    base = (t // 4 % 2) * 4096 + (t % 4) * 1024
                    return x_sb[:, base:base + 1024]

                @blk1.sync
                def _(sy):
                    for gd in range(16):
                        t0 = gd * 4
                        src = x_own if t0 < NOWN else x_oth
                        r0 = (t0 % NOWN) * 128
                        if gd >= 2:
                            sy.wait_ge(c_a1, 4 * (t0 - 5) + 4)
                            if gd <= NOWN // 4 + 1:
                                sy.wait_ge(c_xe[0], 16 * (gd - 1))
                        sy.dma_start(
                            out=x_sb[:, (gd % 2) * 4096:(gd % 2 + 1) * 4096]
                            .rearrange("p (c f) -> p c f", f=1024),
                            in_=src[r0:r0 + 512, :].rearrange(
                                "(c p) f -> p c f", p=128),
                        ).then_inc(c_x[gd % 2], 16)
                        if gd >= 1:
                            for t in range(4 * (gd - 1), 4 * (gd - 1) + 4):
                                if t < NOWN:
                                    sy.wait_ge(c_e1, 4 * t + 1)
                                    if t >= 2:
                                        sy.wait_ge(c_sp[t % 2], 16 * (t // 2))
                                    sy.dma_start(
                                        out=xnsp[:, :, t * 128:(t + 1) * 128]
                                        .rearrange("j p t -> p j t"),
                                        in_=xn_fm[:, (t % 4) * 1024:
                                                  (t % 4 + 1) * 1024]
                                        .rearrange("p (j t) -> p j t", t=128),
                                    ).then_inc(c_sp[t % 2], 16)

                @blk1.vector
                def _(v):
                    for it in range(NCH + 2):
                        if it < NCH:
                            t = it
                            xsl = xsl_of(t)
                            v.wait_ge(c_x[(t // 4) % 2], 16 * (t // 8 + 1))
                            if t >= 4:
                                v.wait_ge(c_a1, 4 * (t - 4) + 4)
                            v.reduce_sum(stc(t, 0), xsl, axis=AX.X,
                                         ).then_inc(c_d1, 1)
                            v.wait_ge(c_d1, d1_red[t])
                            v.tensor_scalar(stc(t, 1), stc(t, 0), -1.0 / E,
                                            None, ALU.mult).then_inc(c_d1, 1)
                        if 1 <= it <= NCH:
                            t = it - 1
                            v.wait_ge(c_a1, 4 * t + 1)
                            v.tensor_scalar(stc(t, 2), stc(t, 5), 1.0 / E,
                                            LN_EPS, ALU.mult, ALU.add,
                                            ).then_inc(c_d1, 1)
                            v.wait_ge(c_a1, 4 * t + 3)
                            v.tensor_tensor(stc(t, 4), stc(t, 1), stc(t, 3),
                                            ALU.mult).then_inc(c_d1, 1)
                        if it >= 2:
                            t = it - 2
                            v.wait_ge(c_pe1, 11 * t + 8)
                            if t >= 4:
                                v.wait_ge(c_pe1, 11 * (t - 4) + 11)
                                v.wait_ge(c_sp[t % 2], sp_par_cnt(t - 4))
                            v.tensor_copy(xn_fm[:, (t % 4) * 1024:
                                                (t % 4 + 1) * 1024],
                                          ps_tr[:, (t % 2) * 1024:
                                                (t % 2 + 1) * 1024],
                                          ).then_inc(c_e1, 1)
                            v.wait_ge(c_pe1, 11 * t + 9)
                            v.tensor_copy(q2[:, t * 128:(t + 1) * 128],
                                          ps_qq[:, :]).then_inc(c_e1, 1)
                            v.wait_ge(c_pe1, 11 * t + 10)
                            v.tensor_copy(k2[:, t * 128:(t + 1) * 128],
                                          ps_kk[:, :]).then_inc(c_e1, 1)
                            v.wait_ge(c_pe1, 11 * t + 11)
                            v.tensor_copy(v_sb[:, t * 64:(t + 1) * 64],
                                          ps_v[:, :]).then_inc(c_e1, 1)

                @blk1.scalar
                def _(s_):
                    for t in range(NCH):
                        xsl = xsl_of(t)
                        s_.wait_ge(c_x[(t // 4) % 2], 16 * (t // 8 + 1))
                        s_.wait_ge(c_d1, d1_negm[t])
                        if t >= 4:
                            s_.wait_ge(c_d1, d1_mb[t - 4])
                        if t >= 2:
                            s_.wait_ge(c_a1, 4 * (t - 2) + 4)
                        s_.activation(sq_scr[:, (t % 2) * 1024:(t % 2 + 1) * 1024],
                                      xsl, AF.Square, bias=stc(t, 1), scale=1.0,
                                      accum_out=stc(t, 5)).then_inc(c_a1, 1)
                        s_.wait_ge(c_d1, d1_var[t])
                        s_.activation(stc(t, 6), stc(t, 2), AF.Ln,
                                      ).then_inc(c_a1, 1)
                        s_.wait_ge(c_a1, 4 * t + 2)
                        s_.activation(stc(t, 3), stc(t, 6), AF.Exp, scale=-0.5,
                                      ).then_inc(c_a1, 1)
                        s_.wait_ge(c_d1, d1_mb[t])
                        s_.wait_ge(c_a1, 4 * t + 3)
                        if t >= 4:
                            s_.wait_ge(c_pe1, 11 * (t - 4) + 8)
                        s_.activation(xn_tm[:, (t % 4) * 1024:(t % 4 + 1) * 1024],
                                      xsl, AF.Identity, bias=stc(t, 4),
                                      scale=stc(t, 3)).then_inc(c_a1, 1)

                @blk1.gpsimd
                def _(gg):
                    gg.dma_start(out=wqkv_bf[:, :].rearrange(
                        "p (c n) -> p c n", n=320),
                        in_=wqkv.rearrange("(c p) n -> p c n", p=128),
                    ).then_inc(c_wq, 16)
                    for gd in range(NOWN // 4):
                        gg.wait_ge(c_x[gd % 2], 16 * (gd // 2 + 1))
                        if gd >= 1:
                            gg.wait_ge(c_xe[0], 16 * gd)
                        gg.dma_start(
                            out=out_d[gd * 512:(gd + 1) * 512, :]
                            .rearrange("(c p) f -> p c f", p=128),
                            in_=x_sb[:, (gd % 2) * 4096:(gd % 2 + 1) * 4096]
                            .rearrange("p (c f) -> p c f", f=1024),
                        ).then_inc(c_xe[0], 16)

                @blk1.tensor
                def _(t_):
                    t_.wait_ge(c_gp, 2)
                    t_.wait_ge(c_wq, 16)
                    for t in range(NCH):
                        t_.wait_ge(c_a1, 4 * t + 4)
                        if t >= 2:
                            t_.wait_ge(c_e1, 4 * (t - 2) + 1)
                        for j in range(NE):
                            t_.matmul(
                                ps_tr[:, (t % 2) * 1024 + j * 128:
                                      (t % 2) * 1024 + (j + 1) * 128],
                                xn_tm[:, (t % 4) * 1024 + j * 128:
                                      (t % 4) * 1024 + (j + 1) * 128],
                                ident[:, :], start=True, stop=True,
                            ).then_inc(c_pe1, 1)
                        t_.wait_ge(c_e1, 4 * t + 1)
                        if t >= 1:
                            t_.wait_ge(c_e1, 4 * (t - 1) + 4)
                        xfm = xn_fm[:, (t % 4) * 1024:(t % 4 + 1) * 1024]
                        for j in range(NE):
                            mm = t_.matmul(ps_qq[:, :],
                                           wqkv_bf[:, j * 320: j * 320 + 128],
                                           xfm[:, j * 128:(j + 1) * 128],
                                           start=(j == 0), stop=(j == NE - 1))
                            if j == NE - 1:
                                mm.then_inc(c_pe1, 1)
                        for j in range(NE):
                            mm = t_.matmul(ps_kk[:, :],
                                           wqkv_bf[:, j * 320 + 128:
                                                   j * 320 + 256],
                                           xfm[:, j * 128:(j + 1) * 128],
                                           start=(j == 0), stop=(j == NE - 1))
                            if j == NE - 1:
                                mm.then_inc(c_pe1, 1)
                        for j in range(NE):
                            mm = t_.matmul(ps_v[:, :],
                                           xfm[:, j * 128:(j + 1) * 128],
                                           wqkv_bf[:, j * 320 + 256:
                                                   j * 320 + 320],
                                           start=(j == 0), stop=(j == NE - 1))
                            if j == NE - 1:
                                mm.then_inc(c_pe1, 1)

                blk1.__exit__(None, None, None)

            with ExitStack() as stP:
                ident32 = stP.enter_context(
                    nc.sbuf_tensor("ident32", [128, 128], F32))
                ones_r = stP.enter_context(nc.sbuf_tensor("ones_r", [1, 128],
                                                          F32))
                nrm = stP.enter_context(nc.sbuf_tensor("nrm", [128, 3 * NCH],
                                                       F32))
                sqd = stP.enter_context(nc.sbuf_tensor("sqd", [128, 2 * 64],
                                                       F32))
                tiny = stP.enter_context(nc.sbuf_tensor("tiny", [128, 8], F32))
                trow = stP.enter_context(nc.sbuf_tensor("trow", [1, 256], F32))
                qbar = stP.enter_context(nc.sbuf_tensor("qbar", [64, 2], F32))
                qbarb = stP.enter_context(nc.sbuf_tensor("qbarb", [64, 1],
                                                         BF16))
                mc = stP.enter_context(nc.sbuf_tensor("mc", [128, NCH], F32))
                ps_tm = stP.enter_context(nc.psum_tensor("ps_tm", [128, 128],
                                                         F32))
                ps_r = stP.enter_context(nc.psum_tensor("ps_r", [1, 128], F32))
                ps_b = stP.enter_context(nc.psum_tensor("ps_b", [128, 1], F32))
                ps_mc = stP.enter_context(nc.psum_tensor("ps_mc", [128, 2],
                                                         F32))
                blk2 = nc.Block()
                blk2.__enter__()

                @blk2.gpsimd
                def _(gg):
                    gg.memset(ident32[:, :], 0.0).then_inc(c_gp, 1)
                    gg.wait_ge(c_gp, GP0 + 1)
                    gg.affine_select(out=ident32[:, :], in_=ident32[:, :],
                                     compare_op=ALU.not_equal, fill=1.0,
                                     base=0, pattern=[[-1, 128]],
                                     channel_multiplier=1).then_inc(c_gp, 1)
                    gg.memset(ones_r[0:1, :], 1.0).then_inc(c_gp, 1)

                @blk2.tensor
                def _(t_):
                    for which, src in ((0, q2), (1, k2)):
                        for kk in range(NCH):
                            u = which * NCH + kk
                            if u >= 2:
                                t_.wait_ge(c_na, u - 1)
                            t_.matmul(ps_tm[:, (u % 2) * 64:(u % 2) * 64 + 64],
                                      src[0:64, kk * 128:(kk + 1) * 128],
                                      ident[0:64, 0:64], start=True, stop=True,
                                      ).then_inc(c_nm, 1)
                    t_.wait_ge(c_nd, 1)
                    t_.wait_ge(c_gp, GP0 + 2)
                    t_.matmul(ps_r[0:1, 0:128], tiny[:, 0:1], ident32[:, :],
                              start=True, stop=True).then_inc(c_nm, 1)
                    t_.wait_ge(c_nd, 4)
                    t_.wait_ge(c_gp, GP0 + 3)
                    t_.matmul(ps_b[:, 0:1], ones_r[0:1, :], trow[0:1, 1:2],
                              start=True, stop=True).then_inc(c_nm, 1)
                    t_.wait_ge(c_nd, 7)
                    for kk in range(NCH):
                        if kk >= 2:
                            t_.wait_ge(c_nd, 9 + kk - 2 + 1)
                        t_.matmul(ps_mc[:, kk % 2: kk % 2 + 1],
                                  k2[0:64, kk * 128:(kk + 1) * 128],
                                  qbarb[0:64, 0:1], start=True, stop=True,
                                  ).then_inc(c_nm, 1)

                @blk2.scalar
                def _(s_):
                    for u in range(2 * NCH):
                        s_.wait_ge(c_nm, u + 1)
                        s_.activation(sqd[:, (u % 2) * 64:(u % 2) * 64 + 64],
                                      ps_tm[:, (u % 2) * 64:(u % 2) * 64 + 64],
                                      AF.Square,
                                      accum_out=nrm[:, u:u + 1],
                                      ).then_inc(c_na, 1)
                    s_.wait_ge(c_nd, 3)
                    s_.activation(trow[0:1, 0:1], trow[0:1, 2:3], AF.Ln,
                                  ).then_inc(c_na, 1)
                    s_.wait_ge(c_na, 2 * NCH)
                    s_.activation(nrm[:, 2 * NCH:3 * NCH], nrm[:, NCH:2 * NCH],
                                  AF.Ln).then_inc(c_na, 1)
                    s_.wait_ge(c_nd, 8)
                    s_.wait_ge(c_na, 2 * NCH + 2)
                    s_.activation(negb[:, :], nrm[:, 2 * NCH:3 * NCH], AF.Exp,
                                  scale=0.5, bias=tiny[:, 1:2],
                                  ).then_inc(c_na, 1)

                @blk2.vector
                def _(v):
                    v.wait_ge(c_na, NCH)
                    v.reduce_max(tiny[:, 0:1], nrm[:, 0:NCH], axis=AX.X,
                                 ).then_inc(c_nd, 1)
                    v.wait_ge(c_nm, 129)
                    v.tensor_copy(trow[0:1, 128:256], ps_r[0:1, 0:128],
                                  ).then_inc(c_nd, 1)
                    v.wait_ge(c_nd, 2)
                    v.reduce_max(trow[0:1, 2:3], trow[0:1, 128:256], axis=AX.X,
                                 ).then_inc(c_nd, 1)
                    v.wait_ge(c_na, 2 * NCH + 1)
                    v.tensor_scalar(trow[0:1, 1:2], trow[0:1, 0:1], 0.5, None,
                                    ALU.mult).then_inc(c_nd, 1)
                    v.reduce_sum(qbar[0:64, 0:1], q2[0:64, :], axis=AX.X,
                                 ).then_inc(c_nd, 1)
                    v.wait_ge(c_nd, 5)
                    v.tensor_scalar(qbar[0:64, 1:2], qbar[0:64, 0:1], 1.0 / S,
                                    None, ALU.mult).then_inc(c_nd, 1)
                    v.wait_ge(c_nd, 6)
                    v.tensor_copy(qbarb[0:64, 0:1], qbar[0:64, 1:2],
                                  ).then_inc(c_nd, 1)
                    v.wait_ge(c_nm, 130)
                    v.tensor_copy(tiny[:, 1:2], ps_b[:, 0:1]).then_inc(c_nd, 1)
                    for kk in range(NCH):
                        v.wait_ge(c_nm, 130 + kk + 1)
                        v.tensor_copy(mc[:, kk:kk + 1],
                                      ps_mc[:, kk % 2: kk % 2 + 1],
                                      ).then_inc(c_nd, 1)
                    v.wait_ge(c_na, 2 * NCH + 3)
                    v.wait_ge(c_nd, 72)
                    v.tensor_tensor(gmS[:, :], negb[:, :], mc[:, :],
                                    ALU.subtract).then_inc(c_nd, 1)
                    v.wait_ge(c_nd, 73)
                    v.tensor_scalar(gmS[:, :], gmS[:, :], 1e-3, None,
                                    ALU.max).then_inc(c_nd, 1)
                    v.wait_ge(c_nd, 74)
                    v.reciprocal(sk[:, :], gmS[:, :]).then_inc(c_nd, 1)
                    v.wait_ge(c_nd, 75)
                    v.tensor_scalar(sk[:, :], sk[:, :], SC1, None,
                                    ALU.mult).then_inc(c_nd, 1)
                    v.wait_ge(c_nd, 76)
                    v.tensor_scalar(gmS[:, :], gmS[:, :], 1.0 / SC1, None,
                                    ALU.mult).then_inc(c_nd, 1)
                    v.wait_ge(c_nd, 77)
                    v.tensor_tensor(sbias[:, :], negb[:, :], sk[:, :],
                                    ALU.mult).then_inc(c_nd, 1)
                    v.wait_ge(c_nd, 78)
                    v.tensor_scalar(sbias[:, :], sbias[:, :], -1.0, None,
                                    ALU.mult).then_inc(c_nd, 1)
                    v.wait_ge(c_nd, 79)
                    v.tensor_scalar(negb[:, :], negb[:, :], -1.0, None,
                                    ALU.mult).then_inc(c_nd, 1)
                    v.wait_ge(c_nd, 80)
                    v.tensor_scalar(sbias[:, :], sbias[:, :], 60.0, None,
                                    ALU.add).then_inc(c_nd, 1)
                    v.wait_ge(c_nd, 81)
                    v.scalar_tensor_tensor(negb[:, :], gmS[:, :], 60.0,
                                           negb[:, :], ALU.mult, ALU.add,
                                           ).then_inc(c_nd, 1)

                blk2.__exit__(None, None, None)

            with ExitStack() as stW:
                wg_bf = stW.enter_context(
                    nc.sbuf_tensor("wg_bf", [128, NE * JH * 128], BF16))
                wu_bf = stW.enter_context(
                    nc.sbuf_tensor("wu_bf", [128, NE * JH * 128], BF16))
                wd_bf = stW.enter_context(
                    nc.sbuf_tensor("wd_bf", [128, JH * 1024], BF16))
                xn_st = stW.enter_context(
                    nc.sbuf_tensor("xn_st", [128, 2 * 2048], BF16))
                graw = stW.enter_context(nc.sbuf_tensor("graw", [128, 3072],
                                                        F32))
                uraw = stW.enter_context(nc.sbuf_tensor("uraw", [128, 3072],
                                                        F32))
                hh = stW.enter_context(nc.sbuf_tensor("hh", [128, 3072], BF16))
                o_st = stW.enter_context(
                    nc.sbuf_tensor("o_st", [128, 2 * 2048], F32))

                def cast_w(gg, half, cwg, cwu, cwd):
                    j0 = half * JH
                    for cch in range(NE):
                        gg.dma_start(
                            out=wg_bf[:, cch * JH * 128:(cch + 1) * JH * 128],
                            in_=wg_t[cch * 128:(cch + 1) * 128,
                                     j0 * 128:(j0 + JH) * 128],
                        ).then_inc(cwg, 16)
                    for cch in range(NE):
                        gg.dma_start(
                            out=wu_bf[:, cch * JH * 128:(cch + 1) * JH * 128],
                            in_=wu_t[cch * 128:(cch + 1) * 128,
                                     j0 * 128:(j0 + JH) * 128],
                        ).then_inc(cwu, 16)
                    for gi in range(4):
                        jj = j0 + gi * 3
                        gg.dma_start(
                            out=wd_bf[:, gi * 3 * 1024:(gi + 1) * 3 * 1024]
                            .rearrange("p (c d) -> p c d", d=1024),
                            in_=wd_t[jj * 128:(jj + 3) * 128, :].rearrange(
                                "(c p) d -> p c d", p=128),
                        ).then_inc(cwd, 16)

                with ExitStack() as stF:
                    e_sb = stF.enter_context(
                        nc.sbuf_tensor("e_sb", [128, 2 * HALF], BF16))
                    edmp = stF.enter_context(
                        nc.sbuf_tensor("edmp", [128, 1024], BF16))
                    wp = stF.enter_context(nc.sbuf_tensor("wp", [128, 32], F32))
                    zp = stF.enter_context(nc.sbuf_tensor("zp", [128, 32], F32))
                    zz = stF.enter_context(nc.sbuf_tensor("zz", [128, 8], F32))
                    mxp = stF.enter_context(nc.sbuf_tensor("mxp", [128, 16], F32))
                    mxt = stF.enter_context(nc.sbuf_tensor("mxt", [128, 2], F32))
                    o_fm = stF.enter_context(
                        nc.sbuf_tensor("o_fm", [64, HALF], BF16))
                    vhat = stF.enter_context(
                        nc.sbuf_tensor("vhat", [128, 8 * 128], BF16))
                    ps_a = stF.enter_context(
                        nc.psum_tensor("ps_a", [128, 2048], F32))
                    ps_o = stF.enter_context(nc.psum_tensor("ps_o", [128, 512],
                                                             F32))
                    ps_gu = [stF.enter_context(
                        nc.psum_tensor("ps_gu%d" % i, [128, 512], F32))
                        for i in range(2)]
                    ps_y = stF.enter_context(nc.psum_tensor("ps_y", [128, 512],
                                                            F32))
                    blkF = nc.Block()
                    blkF.__enter__()

                    @blkF.tensor
                    def _(t_):
                        ui = [0]

                        _cons_sem = {"ae1": c_ae1, "ae2": c_ae2, "wc": c_wc}

                        def pe_unit():
                            i = ui[0]
                            u = units[i]
                            if i >= 2:
                                chn, cnt = units[i - 2]["cons"]
                                t_.wait_ge(_cons_sem[chn], cnt)
                            if u["pass_"] == 2 and u["g"] == 0:
                                t_.wait_ge(c_wc, wc_ng[2 * u["p"] + 1])
                            ch = c_pa1 if u["pass_"] == 1 else c_pa2
                            slot = i % 2
                            for hn in range(2):
                                n = u["npair"] * 2 + hn
                                t_.matmul(
                                    ps_a[:, slot * 1024 + hn * 512:
                                         slot * 1024 + (hn + 1) * 512],
                                    k2[:, u["kk"] * 128:(u["kk"] + 1) * 128],
                                    q2[:, n * 512:(n + 1) * 512],
                                    start=True, stop=True,
                                ).then_inc(ch, 1)
                            ui[0] += 1

                        def pe_gate(s, j):
                            if s == 0 and j == 0:
                                t_.wait_ge(c_wgA, 128)
                            if j == 0:
                                t_.wait_ge(c_xn[s % 2], 16 * (s // 2 + 1))
                            if 12 * s + j >= 2:
                                t_.wait_ge(c_ge, 12 * s + j - 1)
                                t_.wait_ge(c_ue, 12 * s + j - 1)
                            xs = xn_st[:, (s % 2) * 2048:(s % 2 + 1) * 2048]
                            for e in range(NE):
                                mm = t_.matmul(
                                    ps_gu[j % 2][:, 0:256],
                                    wg_bf[:, e * JH * 128 + j * 128:
                                          e * JH * 128 + (j + 1) * 128],
                                    xs[:, e * 256:(e + 1) * 256],
                                    start=(e == 0), stop=False)
                                if e == NE - 1:
                                    mm.then_inc(c_pg, 1)

                        def pe_up(s, j):
                            if s == 0 and j == 0:
                                t_.wait_ge(c_wuA, 128)
                            if 12 * s + j >= 2:
                                t_.wait_ge(c_ue, 12 * s + j - 1)
                            xs = xn_st[:, (s % 2) * 2048:(s % 2 + 1) * 2048]
                            for e in range(NE):
                                mm = t_.matmul(
                                    ps_gu[j % 2][:, 256:512],
                                    wu_bf[:, e * JH * 128 + j * 128:
                                          e * JH * 128 + (j + 1) * 128],
                                    xs[:, e * 256:(e + 1) * 256],
                                    start=False, stop=(e == NE - 1))
                                if e == NE - 1:
                                    mm.then_inc(c_pu, 1)

                        def pe_down(s, rr, j):
                            mt, dh = rr % 2, rr // 2
                            if j == 0:
                                if s == 0 and rr == 0:
                                    t_.wait_ge(c_wdA, 64)
                                t_.wait_ge(c_d3,
                                           4 * s + (1 if rr == 0 else 4))
                                if 4 * s + rr >= 1:
                                    t_.wait_ge(c_ep, 4 * s + rr)
                                if rr == 2 and s >= 1:
                                    t_.wait_ge(c_do, (2 * s - 1) * 8 + 8)
                            if rr == 0 and j in (3, 6, 9):
                                t_.wait_ge(c_d3, 4 * s + j // 3 + 1)
                            mm = t_.matmul(
                                ps_y[:, :],
                                hh[:, j * 256 + mt * 128:
                                   j * 256 + mt * 128 + 128],
                                wd_bf[:, j * 1024 + dh * 512:
                                      j * 1024 + dh * 512 + 512],
                                start=(j == 0), stop=(j == JH - 1))
                            if j == JH - 1:
                                mm.then_inc(c_py, 1)

                        def pe_oitem(p, n):
                            if n == 0:
                                t_.wait_ge(c_ae2, ae2_after_pair[p])
                                t_.wait_ge(c_vh, 3 * (2 * p + 1) + 3)
                            alt = (p % 2 == 1)
                            tgt = ps_y if (alt and n % 2 == 1) else ps_o
                            gidx = p * 8 + n
                            if alt:
                                if n == 1:
                                    t_.wait_ge(c_ep,
                                               64 if p == 31
                                               else 4 * ((p + 1) // 2) + 2)
                                if gidx >= 2:
                                    t_.wait_ge(c_do, gidx - 1)
                            else:
                                if gidx >= 1:
                                    t_.wait_ge(c_do, gidx)
                            for ci in range(2):
                                kkc = 2 * p + ci
                                mm = t_.matmul(
                                    tgt[:, :],
                                    vhat[:, (kkc % 8) * 128:(kkc % 8 + 1) * 128],
                                    e_sb[:, (kkc % 2) * HALF + n * 512:
                                         (kkc % 2) * HALF + (n + 1) * 512],
                                    start=(ci == 0), stop=(ci == 1))
                                if ci == 1:
                                    mm.then_inc(c_po, 1)

                        def weave(nunits, fillers):
                            fi = 0
                            nf = len(fillers)
                            for q in range(nunits):
                                pe_unit()
                                want = (q + 1) * nf // nunits
                                while fi < want:
                                    fillers[fi](); fi += 1
                            while fi < nf:
                                fillers[fi](); fi += 1

                        for s in range(NSTR):
                            gu1 = []
                            for j in range(6):
                                gu1.append(lambda s=s, j=j: pe_gate(s, j))
                                gu1.append(lambda s=s, j=j: pe_up(s, j))
                            gu2 = []
                            for j in range(6, JH):
                                gu2.append(lambda s=s, j=j: pe_gate(s, j))
                                gu2.append(lambda s=s, j=j: pe_up(s, j))
                            r = [[(lambda s=s, rr=rr, j=j: pe_down(s, rr, j))
                                  for j in range(JH)] for rr in range(4)]
                            if s == 0:
                                weave(16, [])
                                weave(16, [])
                                weave(16, [])
                                for f in gu1 + gu2:
                                    f()
                                for rr in range(3):
                                    for f in r[rr]:
                                        f()
                            else:
                                oprev = [(lambda p=2 * s - 2, n=n:
                                          pe_oitem(p, n)) for n in range(8)]
                                ocur = [(lambda p=2 * s - 1, n=n:
                                         pe_oitem(p, n)) for n in range(8)]
                                r3prev = [(lambda sp=s - 1, j=j:
                                           pe_down(sp, 3, j))
                                          for j in range(JH)]
                                seg1f = (gu1[0:2] + oprev[0:4] + gu1[2:4]
                                         + oprev[4:8] + gu1[4:] + r3prev)
                                weave(16, seg1f)
                                weave(16, gu2)
                                for f in r[0]:
                                    f()
                                weave(16, r[1])
                                for f in ocur:
                                    f()
                                weave(16, r[2])
                        for n in range(8):
                            pe_oitem(30, n)
                        for j in range(JH):
                            pe_down(NSTR - 1, 3, j)
                        for q in range(16):
                            pe_unit()
                        for n in range(8):
                            pe_oitem(31, n)

                    @blkF.scalar
                    def _(s_):
                        prev_exp = [None]

                        def act_exp(i):
                            u = units[i]
                            slot = i % 2
                            kk, npair = u["kk"], u["npair"]
                            if prev_exp[0] is not None:
                                s_.wait_ge(*prev_exp[0])
                            prev_exp[0] = (c_ae1 if u["pass_"] == 1 else c_ae2,
                                           u["exp_after"])
                            if u["pass_"] == 1:
                                s_.wait_ge(c_pa1, u["mm_after"])
                                if kk >= 4 and npair == 0:
                                    s_.wait_ge(c_wc, wc_ws[kk - 4])
                                s_.activation(
                                    edmp[:, :],
                                    ps_a[:, slot * 1024:(slot + 1) * 1024],
                                    AF.Exp, scale=sk[:, kk:kk + 1],
                                    bias=sbias[:, kk:kk + 1],
                                    accum_out=wp[:, (kk % 4) * 8 + npair:
                                                 (kk % 4) * 8 + npair + 1],
                                ).then_inc(c_ae1, 1)
                            else:
                                s_.wait_ge(c_pa2, u["mm_after"])
                                if npair == 0:
                                    if kk >= 2:
                                        s_.wait_ge(c_po, 8 * (kk // 2))
                                    if kk >= 4:
                                        s_.wait_ge(c_vh, 3 * (kk - 4) + 1)
                                if npair < 4:
                                    out = e_sb[:, (kk % 2) * HALF + npair * 1024:
                                               (kk % 2) * HALF
                                               + (npair + 1) * 1024]
                                else:
                                    out = edmp[:, :]
                                s_.activation(
                                    out, ps_a[:, slot * 1024:(slot + 1) * 1024],
                                    AF.Exp, scale=1.0, bias=negmt[:, kk:kk + 1],
                                    accum_out=zp[:, (kk % 4) * 8 + npair:
                                                 (kk % 4) * 8 + npair + 1],
                                ).then_inc(c_ae2, 1)

                        def act_ln(kk):
                            s_.wait_ge(c_wc, wc_ws[kk])
                            s_.activation(zz[:, 4 + kk % 2: 5 + kk % 2],
                                          zz[:, 4 + kk % 2: 5 + kk % 2],
                                          AF.Ln, bias=eps86[:, 0:1],
                                          ).then_inc(c_ln, 1)

                        def act_silu(s, h):
                            s_.wait_ge(c_ge, 12 * s + 3 * (h + 1))
                            s_.activation(graw[:, h * 768:(h + 1) * 768],
                                          graw[:, h * 768:(h + 1) * 768],
                                          AF.Silu, scale=SCALE,
                                          ).then_inc(c_a3, 1)

                        def exps_of(pass_, p):
                            i0 = pass_start[(pass_, p)]
                            for q in range(16):
                                act_exp(i0 + q)

                        for s in range(NSTR):
                            exps_of(1, 2 * s)
                            act_ln(4 * s); act_ln(4 * s + 1)
                            if s >= 1:
                                exps_of(2, 2 * s - 1)
                                for hq in range(4):
                                    act_silu(s, hq)
                                exps_of(2, 2 * s)
                            else:
                                exps_of(2, 0)
                                for hq in range(4):
                                    act_silu(0, hq)
                        exps_of(2, 31)

                    @blkF.vector
                    def _(v):
                        def evg(s, j):
                            v.wait_ge(c_pg, 12 * s + j + 1)
                            v.wait_ge(c_pu, 12 * s + j + 1)
                            if j == 0 and s >= 1:
                                v.wait_ge(c_a3, 4 * s)
                                v.wait_ge(c_d3, 4 * s)
                            v.tensor_copy(graw[:, j * 256:(j + 1) * 256],
                                          ps_gu[j % 2][:, 0:256],
                                          ).then_inc(c_ge, 1)

                        def evu(s, j):
                            v.wait_ge(c_pu, 12 * s + j + 1)
                            if j == 0 and s >= 1:
                                v.wait_ge(c_d3, 4 * s)
                            v.tensor_copy(uraw[:, j * 256:(j + 1) * 256],
                                          ps_gu[j % 2][:, 256:512],
                                          ).then_inc(c_ue, 1)

                        def wsum(kk):
                            v.wait_ge(c_ae1, ae1_after_chunk[kk])
                            if kk >= 4:
                                v.wait_ge(c_wc, wc_t1[kk - 4])
                                v.wait_ge(c_ln, ln_cnt[kk - 4])
                            v.reduce_sum(zz[:, 4 + kk % 2: 5 + kk % 2],
                                         wp[:, (kk % 4) * 8:(kk % 4) * 8 + 8],
                                         axis=AX.X).then_inc(c_wc, 1)

                        def negm(kk):
                            v.wait_ge(c_ln, ln_cnt[kk])
                            if kk >= 4:
                                v.wait_ge(c_wc, wc_ng[kk - 4])
                            v.tensor_tensor(zz[:, 6 + kk % 2: 7 + kk % 2],
                                            zz[:, 4 + kk % 2: 5 + kk % 2],
                                            gmS[:, kk:kk + 1], ALU.mult,
                                            ).then_inc(c_wc, 1)
                            v.wait_ge(c_wc, wc_t1[kk])
                            v.tensor_tensor(negmt[:, kk:kk + 1],
                                            negb[:, kk:kk + 1],
                                            zz[:, 6 + kk % 2: 7 + kk % 2],
                                            ALU.subtract).then_inc(c_wc, 1)

                        def dmax(kk, npair):
                            i = pass_start[(1, kk // 2)] + npair * 2 + (kk % 2)
                            v.wait_ge(c_pa1, units[i]["mm_after"])
                            if kk >= 6:
                                v.wait_ge(c_wc, wc_mxw[kk - 4])
                            v.reduce_max(mxp[:, (kk % 2) * 8 + npair:
                                              (kk % 2) * 8 + npair + 1],
                                         ps_a[:, (i % 2) * 1024:
                                              (i % 2 + 1) * 1024],
                                         axis=AX.X).then_inc(c_wc, 1)

                        def mxw_op(kk):
                            v.wait_ge(c_wc, wc_mx[(kk, 7)])
                            v.reduce_max(mxt[:, kk % 2: kk % 2 + 1],
                                         mxp[:, (kk % 2) * 8:(kk % 2) * 8 + 8],
                                         axis=AX.X).then_inc(c_wc, 1)

                        def ngD(kk):
                            v.wait_ge(c_wc, wc_mxw[kk])
                            v.tensor_scalar(negmt[:, kk:kk + 1],
                                            mxt[:, kk % 2: kk % 2 + 1], -1.0,
                                            None, ALU.mult).then_inc(c_wc, 1)

                        def dmax_pass(p):
                            for gg in range(16):
                                dmax(2 * p + (gg % 2), gg // 2)
                            mxw_op(2 * p); ngD(2 * p)
                            mxw_op(2 * p + 1); ngD(2 * p + 1)

                        def zchain(kk):
                            v.wait_ge(c_ae2, ae2_after_chunk[kk])
                            if kk >= 2:
                                v.wait_ge(c_vh, 3 * (kk - 2) + 2)
                            v.reduce_sum(zz[:, kk % 2: kk % 2 + 1],
                                         zp[:, (kk % 4) * 8:(kk % 4) * 8 + 8],
                                         axis=AX.X).then_inc(c_vh, 1)
                            v.wait_ge(c_vh, 3 * kk + 1)
                            v.reciprocal(zz[:, 2 + kk % 2: 3 + kk % 2],
                                         zz[:, kk % 2: kk % 2 + 1],
                                         ).then_inc(c_vh, 1)
                            v.wait_ge(c_vh, 3 * kk + 2)
                            if kk >= 8:
                                v.wait_ge(c_po, 8 * ((kk - 8) // 2 + 1))
                            else:
                                v.wait_ge(c_gp, GP0 + 4)
                            v.tensor_scalar(vhat[:, (kk % 8) * 128:
                                                 (kk % 8) * 128 + 64],
                                            v_sb[:, kk * 64:(kk + 1) * 64],
                                            zz[:, 2 + kk % 2: 3 + kk % 2], None,
                                            ALU.mult).then_inc(c_vh, 1)

                        def drain1(p, n):
                            v.wait_ge(c_po, p * 8 + n + 1)
                            srcb = (ps_y if (p % 2 == 1 and n % 2 == 1)
                                    else ps_o)
                            if p == 0:
                                v.tensor_copy(
                                    o_fm[0:64, n * 512:(n + 1) * 512],
                                    srcb[0:64, :]).then_inc(c_do, 1)
                            else:
                                v.tensor_tensor(
                                    o_fm[0:64, n * 512:(n + 1) * 512],
                                    o_fm[0:64, n * 512:(n + 1) * 512],
                                    srcb[0:64, :], ALU.add,
                                    ).then_inc(c_do, 1)

                        def drains(p):
                            for n in range(8):
                                drain1(p, n)

                        def hhop(s, h):
                            v.wait_ge(c_a3, 4 * s + h + 1)
                            v.wait_ge(c_ue, 12 * s + 3 * (h + 1))
                            if s >= 1 and h == 0:
                                v.wait_ge(c_py, 4 * s)
                            v.tensor_tensor(hh[:, h * 768:(h + 1) * 768],
                                            graw[:, h * 768:(h + 1) * 768],
                                            uraw[:, h * 768:(h + 1) * 768],
                                            ALU.mult).then_inc(c_d3, 1)

                        def stt_A(s, rr):
                            mt, dh = rr % 2, rr // 2
                            v.wait_ge(c_py, 4 * s + rr + 1)
                            if s >= 2:
                                v.wait_ge(c_out[s % 2], 16 * (s // 2))
                            off = (s % 2) * 2048 + mt * 1024 + dh * 512
                            v.tensor_scalar(
                                o_st[:, off:off + 512], ps_y[:, :], S2,
                                None, ALU.mult).then_inc(c_ep, 1)

                        def drains2(p, half):
                            for n in (range(4) if half == 0 else range(4, 8)):
                                drain1(p, n)

                        for s in range(NSTR):
                            if s == 0:
                                wsum(0); wsum(1)
                                negm(0); negm(1)
                                dmax_pass(1)
                                zchain(0); zchain(1)
                                for j in range(JH):
                                    evg(0, j); evu(0, j)
                                for hq in range(4):
                                    hhop(0, hq)
                                stt_A(0, 0); stt_A(0, 1)
                                stt_A(0, 2)
                            else:
                                evg(s, 0); evu(s, 0)
                                drains2(2 * s - 2, 0)
                                evg(s, 1); evu(s, 1)
                                drains2(2 * s - 2, 1)
                                for j in range(2, 6):
                                    evg(s, j); evu(s, j)
                                wsum(4 * s); wsum(4 * s + 1)
                                stt_A(s - 1, 3)
                                for j in range(6, JH):
                                    evg(s, j); evu(s, j)
                                zchain(4 * s - 2); zchain(4 * s - 1)
                                hhop(s, 0); hhop(s, 1)
                                negm(4 * s); negm(4 * s + 1)
                                hhop(s, 2); hhop(s, 3)
                                stt_A(s, 0)
                                dmax_pass(2 * s + 1)
                                stt_A(s, 1)
                                drains(2 * s - 1)
                                zchain(4 * s); zchain(4 * s + 1)
                                stt_A(s, 2)
                        drains(30)
                        stt_A(NSTR - 1, 3)
                        zchain(62); zchain(63)
                        drains(31)
                        v.wait_ge(c_do, 256)
                        v.tensor_copy(o_bf[0:64, :], o_fm[0:64, :],
                                      ).then_inc(c_do, 1)

                    @blkF.gpsimd
                    def _(gg):
                        gg.memset(vhat[:, :], 0.0).then_inc(c_gp, 1)
                        cast_w(gg, 0, c_wgA, c_wuA, c_wdA)
                        gg.wait_ge(c_xe[0], 64)
                        for s in range(NSTR):
                            gg.wait_ge(c_ep, 4 * s + 4)
                            gg.dma_start(
                                out=out_d[s * 256:(s + 1) * 256, :].rearrange(
                                    "(c p) f -> p c f", p=128),
                                in_=o_st[:, (s % 2) * 2048:(s % 2 + 1) * 2048]
                                .rearrange("p (c f) -> p c f", f=1024),
                                accum_op=ALU.add,
                            ).then_inc(c_out[s % 2], 16)

                    @blkF.sync
                    def _(sy):
                        sy.wait_ge(c_sp[0], 256)
                        sy.wait_ge(c_sp[1], 256)
                        for s in range(NSTR):
                            if s >= 2:
                                sy.wait_ge(c_pu, 12 * (s - 2) + 12)
                                sy.wait_ge(c_xn[s % 2], 16 * (s // 2))
                            sy.dma_start(
                                out=xn_st[:, (s % 2) * 2048:(s % 2 + 1) * 2048]
                                .rearrange("p (j t) -> p j t", t=256),
                                in_=xnsp[:, :, s * 256:(s + 1) * 256].rearrange(
                                    "j p t -> p j t"),
                            ).then_inc(c_xn[s % 2], 16)

                    blkF.__exit__(None, None, None)

                with ExitStack() as stB:
                    o_tm = stB.enter_context(
                        nc.sbuf_tensor("o_tm", [128, 2048], F32))
                    o_stB = stB.enter_context(
                        nc.sbuf_tensor("o_stB", [128, 2 * 2048], F32))
                    ps_gB = [stB.enter_context(
                        nc.psum_tensor("ps_gB%d" % i, [128, 256], F32))
                        for i in range(2)]
                    ps_uB = [stB.enter_context(
                        nc.psum_tensor("ps_uB%d" % i, [128, 256], F32))
                        for i in range(2)]
                    ps_yB = [stB.enter_context(
                        nc.psum_tensor("ps_yB%d" % i, [128, 512], F32))
                        for i in range(2)]
                    ps_ot = [stB.enter_context(
                        nc.psum_tensor("ps_ot%d" % i, [128, 64], F32))
                        for i in range(2)]
                    blkB = nc.Block()
                    blkB.__enter__()

                    @blkB.gpsimd
                    def _(gg):
                        cast_w(gg, 1, c_wgB, c_wuB, c_wdB)
                        for s in range(NSTR):
                            gg.wait_ge(c_ep, 64 + 4 * s + 4)
                            gg.wait_ge(c_out[s % 2], 16 * (s // 2 + 1))
                            gg.dma_start(
                                out=out_d[s * 256:(s + 1) * 256, :].rearrange(
                                    "(c p) f -> p c f", p=128),
                                in_=o_stB[:, (s % 2) * 2048:(s % 2 + 1) * 2048]
                                .rearrange("p (c f) -> p c f", f=1024),
                                accum_op=ALU.add,
                            ).then_inc(c_oa[s % 2], 16)
                        gg.wait_ge(c_oa[0], 128)
                        gg.wait_ge(c_oa[1], 128)

                    @blkB.tensor
                    def _(t_):
                        for c in range(32):
                            if c >= 2:
                                t_.wait_ge(c_ob, c - 1)
                            t_.matmul(ps_ot[c % 2][:, :],
                                      o_bf[0:64, c * 128:(c + 1) * 128],
                                      ident[0:64, 0:64], start=True, stop=True,
                                      ).then_inc(c_po, 1)
                        for s in range(NSTR):
                            t_.wait_ge(c_xn[s % 2], 16 * (8 + s // 2 + 1))
                            xs = xn_st[:, (s % 2) * 2048:(s % 2 + 1) * 2048]
                            for j in range(JH):
                                if s == 0 and j == 0:
                                    t_.wait_ge(c_wgB, 128)
                                    t_.wait_ge(c_wuB, 128)
                                if 12 * s + j >= 2:
                                    t_.wait_ge(c_ge, 192 + 12 * s + j - 1)
                                for e in range(NE):
                                    mm = t_.matmul(
                                        ps_gB[j % 2][:, :],
                                        wg_bf[:, e * JH * 128 + j * 128:
                                              e * JH * 128 + (j + 1) * 128],
                                        xs[:, e * 256:(e + 1) * 256],
                                        start=(e == 0), stop=(e == NE - 1))
                                    if e == NE - 1:
                                        mm.then_inc(c_pg, 1)
                                if 12 * s + j >= 2:
                                    t_.wait_ge(c_ue, 192 + 12 * s + j - 1)
                                for e in range(NE):
                                    mm = t_.matmul(
                                        ps_uB[j % 2][:, :],
                                        wu_bf[:, e * JH * 128 + j * 128:
                                              e * JH * 128 + (j + 1) * 128],
                                        xs[:, e * 256:(e + 1) * 256],
                                        start=(e == 0), stop=(e == NE - 1))
                                    if e == NE - 1:
                                        mm.then_inc(c_pu, 1)
                            for rr in range(4):
                                mt, dh = rr % 2, rr // 2
                                for j in range(JH):
                                    if j == 0:
                                        if s == 0 and rr == 0:
                                            t_.wait_ge(c_wdB, 64)
                                        t_.wait_ge(c_d3, 64 + 4 * s
                                                   + (1 if rr == 0 else 4))
                                        if 4 * s + rr >= 1:
                                            t_.wait_ge(c_ep,
                                                       64 + 4 * s + rr - 1)
                                    if rr == 0 and j in (3, 6, 9):
                                        t_.wait_ge(c_d3,
                                                   64 + 4 * s + j // 3 + 1)
                                    mm = t_.matmul(
                                        ps_yB[rr % 2][:, :],
                                        hh[:, j * 256 + mt * 128:
                                           j * 256 + mt * 128 + 128],
                                        wd_bf[:, j * 1024 + dh * 512:
                                              j * 1024 + dh * 512 + 512],
                                        start=(j == 0), stop=(j == JH - 1))
                                    if j == JH - 1:
                                        mm.then_inc(c_py, 1)

                    @blkB.scalar
                    def _(s_):
                        for s in range(NSTR):
                            for h in range(4):
                                s_.wait_ge(c_ge, 192 + 12 * s + 3 * (h + 1))
                                s_.activation(
                                    graw[:, h * 768:(h + 1) * 768],
                                    graw[:, h * 768:(h + 1) * 768],
                                    AF.Silu, scale=SCALE).then_inc(c_a3, 1)

                    @blkB.vector
                    def _(v):
                        for c in range(32):
                            v.wait_ge(c_po, 256 + c + 1)
                            v.tensor_copy(o_tm[:, c * 64:(c + 1) * 64],
                                          ps_ot[c % 2][:, :]).then_inc(c_ob, 1)
                        for s in range(NSTR):
                            for j in range(JH):
                                v.wait_ge(c_pg, 192 + 12 * s + j + 1)
                                if j == 0 and s >= 1:
                                    v.wait_ge(c_a3, 64 + 4 * s)
                                    v.wait_ge(c_d3, 64 + 4 * s)
                                v.tensor_copy(graw[:, j * 256:(j + 1) * 256],
                                              ps_gB[j % 2][:, :],
                                              ).then_inc(c_ge, 1)
                                v.wait_ge(c_pu, 192 + 12 * s + j + 1)
                                if j == 0 and s >= 1:
                                    v.wait_ge(c_d3, 64 + 4 * s)
                                v.tensor_copy(uraw[:, j * 256:(j + 1) * 256],
                                              ps_uB[j % 2][:, :],
                                              ).then_inc(c_ue, 1)
                                if j in (2, 5, 8, 11):
                                    h = j // 3
                                    v.wait_ge(c_a3, 64 + 4 * s + h + 1)
                                    v.wait_ge(c_ue, 192 + 12 * s + 3 * (h + 1))
                                    if s >= 1 and h == 0:
                                        v.wait_ge(c_py, 64 + 4 * s)
                                    v.tensor_tensor(
                                        hh[:, h * 768:(h + 1) * 768],
                                        graw[:, h * 768:(h + 1) * 768],
                                        uraw[:, h * 768:(h + 1) * 768],
                                        ALU.mult).then_inc(c_d3, 1)
                            for rr in range(4):
                                mt, dh = rr % 2, rr // 2
                                v.wait_ge(c_py, 64 + 4 * s + rr + 1)
                                if s == 0 and rr == 0:
                                    v.wait_ge(c_ob, 32)
                                if s >= 2:
                                    v.wait_ge(c_oa[s % 2], 16 * (s // 2))
                                ch = 2 * s + mt
                                off = (s % 2) * 2048 + mt * 1024 + dh * 512
                                v.scalar_tensor_tensor(
                                    o_stB[:, off:off + 512].rearrange(
                                        "p (r d) -> p r d", d=64),
                                    ps_yB[rr % 2][:, :].rearrange(
                                        "p (r d) -> p r d", d=64),
                                    S2,
                                    o_tm[:, ch * 64:(ch + 1) * 64].rearrange(
                                        "p (r d) -> p r d", r=1)
                                    .broadcast_to([128, 8, 64]),
                                    ALU.mult, ALU.add).then_inc(c_ep, 1)

                    @blkB.sync
                    def _(sy):
                        for s in range(NSTR):
                            if s >= 2:
                                sy.wait_ge(c_pu, 192 + 12 * (s - 2) + 12)
                            sy.wait_ge(c_xn[s % 2], 16 * (8 + s // 2))
                            sy.dma_start(
                                out=xn_st[:, (s % 2) * 2048:(s % 2 + 1) * 2048]
                                .rearrange("p (j t) -> p j t", t=256),
                                in_=xnsp[:, :, s * 256:(s + 1) * 256].rearrange(
                                    "j p t -> p j t"),
                            ).then_inc(c_xn[s % 2], 16)

                    blkB.__exit__(None, None, None)

    return nc


_NC_CACHE = {}


def kernel(L0_attn_x, L0_attn_Wq, L0_attn_Wk, L0_attn_Wv,
           L0_ffn_w_gate, L0_ffn_w_up, L0_ffn_w_down):
    x = np.asarray(L0_attn_x, dtype=np.float32)
    wq = np.asarray(L0_attn_Wq, dtype=np.float32)
    wk = np.asarray(L0_attn_Wk, dtype=np.float32)
    wv = np.asarray(L0_attn_Wv, dtype=np.float32)
    wg = np.asarray(L0_ffn_w_gate, dtype=np.int32)
    wu = np.asarray(L0_ffn_w_up, dtype=np.int32)
    wd = np.asarray(L0_ffn_w_down, dtype=np.int32)

    wqkv = np.ascontiguousarray(
        np.concatenate([wq.T, wq.T, wk.T, np.zeros_like(wk.T), wv.T], axis=1))
    wg_t = np.ascontiguousarray(wg.T)
    wu_t = np.ascontiguousarray(wu.T)
    wd_t = np.ascontiguousarray(wd.T)

    if "nc" not in _NC_CACHE:
        _NC_CACHE["nc"] = build_nc()
    nc = _NC_CACHE["nc"]

    in_maps = []
    for c in range(8):
        b, h = c // 2, c % 2
        xo = np.ascontiguousarray(x[b, h * HALF:(h + 1) * HALF, :])
        xt = np.ascontiguousarray(x[b, (1 - h) * HALF:(2 - h) * HALF, :])
        in_maps.append({
            "x_own": xo, "x_oth": xt, "wqkv": wqkv,
            "wg_t": wg_t, "wu_t": wu_t, "wd_t": wd_t,
        })
    res = run_bass_kernel_spmd(nc, in_maps, core_ids=list(range(8)))
    out = np.empty((B, S, E), dtype=np.float32)
    for c in range(8):
        b, h = c // 2, c % 2
        out[b, h * HALF:(h + 1) * HALF, :] = res.results[c]["out"]
    return out
